# version 23 (final)
# speedup vs baseline: 1.0721x; 1.0038x over previous
import contextlib
import numpy as np

import concourse.bass as bass
import concourse.mybir as mybir
from concourse.bass_utils import run_bass_kernel_spmd

F32 = mybir.dt.float32
F16 = mybir.dt.float16
BF16 = mybir.dt.bfloat16
I32 = mybir.dt.int32
AF = mybir.ActivationFunctionType
ALU = mybir.AluOpType

D = 1024
T = 8192
NB = 4
TH = T // 2
TT = 512
NTILE = TH // TT
DC = D // 128
NST = 16
DTR = 64
MLP = 4096
EPS = 1e-5
NEG = -1.0e30

_DEBUG_TAPS = []
_PHASES = 99

SM = dict(ln1g=0, ln1b=8, ln2g=16, ln2b=24, dtb=32, Dp=40, cxb=48, czb=56,
          b2=64, b1=72, cwx=104, cwz=136)

FDW, FKV, FQ, FY, FAO = 0, 1, 2, 3, 4


class Ctx:
    def __init__(self, nc, es):
        self.nc = nc
        self.es = es
        self.sem = {}
        self.cnt = {}
        self.stacks = {}
        for name in ["qS", "qG", "qC", "qP", "eP", "eA", "eV", "eO", "bar"]:
            self.sem[name] = es.enter_context(nc.semaphore(name))
            self.cnt[name] = 0

    def inc(self, inst, name, k=1):
        inst.then_inc(self.sem[name], k)
        self.cnt[name] += k
        return self.cnt[name]

    def w(self, eng, name, at=None):
        eng.wait_ge(self.sem[name], self.cnt[name] if at is None else at)

    def snap(self):
        return dict(self.cnt)

    def wall(self, eng, snap=None):
        s = snap or self.cnt
        for name in ["qS", "qG", "qC", "eP", "eA", "eV", "eO"]:
            if s[name]:
                eng.wait_ge(self.sem[name], s[name])

    def sb(self, name, shape, dtype=F32, stack=None):
        es = self.stacks[stack] if stack else self.es
        return es.enter_context(self.nc.sbuf_tensor(name, shape, dtype))

    def open_stack(self, name):
        self.stacks[name] = contextlib.ExitStack()
        return self.stacks[name]

    def close_stack(self, name):
        self.stacks.pop(name).close()

    def ps(self, name, shape, dtype=F32):
        return self.es.enter_context(self.nc.psum_tensor(name, shape, dtype))


def build_nc():
    nc = bass.Bass("TRN2", target_bir_lowering=False)
    I = {}

    def din(name, shape, dtype=F32):
        I[name] = nc.dram_tensor(name, list(shape), dtype, kind="ExternalInput").ap()

    din("xT", [NB, D, T])
    din("xhalo", [NB, 3, D])
    din("w_in_T", [D, 2 * D])
    din("smallpm", [128, 256])
    din("xproj_T", [D, 96])
    din("dtproj_T", [DTR, D])
    din("A_log", [D, NST])
    din("outT", [2 * D, D])
    din("w1T", [D, MLP])
    din("w2T", [MLP, D])
    din("sim", [D, NST])
    din("gates", [1, NST])
    din("qTw", [D, D])
    din("kTw", [D, D])
    din("vTw", [D, D])
    din("oTw", [D, D])
    din("masks", [4, 128, TT])
    din("ident", [128, 128], BF16)
    din("hvalid", [2, 4])
    din("eye4", [4, 4])

    O = {}
    O["yout"] = nc.dram_tensor("yout", [D, TH], F32, kind="ExternalOutput").ap()
    O["meta"] = nc.dram_tensor("meta", [1, 4], F32, kind="ExternalOutput").ap()

    DD = {}
    def dloc(name, shape, dtype=F32):
        DD[name] = nc.dram_tensor(name, list(shape), dtype)

    dloc("d_x", [D, TH])
    dloc("d_dwL", [NTILE, 2, D, TT], F16)
    dloc("d_bcL", [NTILE, 32, TT], F16)
    dloc("d_dwS", [16, 2, 512, TT], F16)
    dloc("d_AS", [4, 128, NST])
    dloc("d_yL", [16, 4, 128, TT], F16)
    dloc("d_yM", [2, NTILE, 4, 128, TT], F16)
    dloc("d_kvL", [2, 64, TH], BF16)
    dloc("d_qLo", [64, TH], BF16)
    dloc("d_aoR", [64, TH], BF16)
    dloc("d_a", [NTILE, D, TT])
    dloc("d_u", [NTILE, D, TT])
    dloc("d_xs", [NTILE, D, TT])
    dloc("d_m", [NTILE, D, TT], BF16)
    dloc("d_xin", [NTILE, D, TT])
    dloc("d_halo", [4, 128, 32])
    dloc("d_udp", [NTILE, D, TT], F16)
    dloc("d_zc", [NTILE, D, TT], BF16)
    dloc("d_a2", [NTILE, D, TT])
    dloc("d_h2", [NTILE, D, TT], BF16)
    dloc("d_x1", [NTILE, D, TT])
    dloc("d_fl", [NTILE, D, TT], BF16)
    dloc("d_rn", [NTILE, TT])
    dloc("d_rw", [NTILE, NST, TT], BF16)
    dloc("d_A", [DC, 128, NST])
    dloc("d_w2bf", [MLP, D], BF16)
    dloc("d_rwT", [NST, TH], BF16)
    dloc("bar_in", [1, 16])
    dloc("bar_out", [1, 16])

    def dsh(name, shape, dtype):
        DD[name] = nc.dram_tensor(name, list(shape), dtype, addr_space="Shared")

    dsh("s_dw", [2, NTILE, 2, D, TT], F16)
    dsh("s_bc", [2, NTILE, 32, TT], F16)
    dsh("s_y", [2, 16, 4, 128, TT], F16)
    dsh("s_kv", [2, 64, T], BF16)
    dsh("s_q", [64, T], BF16)
    dsh("s_ao", [64, T], BF16)
    dsh("s_fl", [2, 8], I32)

    for name in _DEBUG_TAPS:
        t = DD[name]
        O["dbg_" + name] = nc.dram_tensor(
            "dbg_" + name, list(t.shape), t.dtype, kind="ExternalOutput").ap()

    with contextlib.ExitStack() as es:
        c = Ctx(nc, es)
        block = es.enter_context(nc.Block())
        st = {}
        emit_p0(c, block, I, O, DD, st)
        if _PHASES >= 1:
            c.close_stack("p0t")
            emit_p1(c, block, I, O, DD, st)
            emit_p1_exchange(c, block, I, O, DD, st)
        if _PHASES >= 2:
            phase_wall(c, block)
            c.close_stack("p1")
            c.close_stack("p01")
            emit_p2(c, block, I, O, DD, st)
        if _PHASES >= 3:
            phase_wall(c, block)
            c.close_stack("p2")
            emit_p3(c, block, I, O, DD, st)
        if _PHASES >= 4:
            emit_p4(c, block, I, O, DD, st)
        if _PHASES >= 5:
            emit_p5(c, block, I, O, DD, st)
        emit_tail(c, block, I, O, DD, st)
        for name in list(c.stacks.keys())[::-1]:
            c.stacks.pop(name).close()
    return nc


def phase_wall(c, block):
    for sec in [block.sync, block.gpsimd, block.scalar, block.vector, block.tensor]:
        @sec
        def _(e):
            c.wall(e)




def emit_p0(c, block, I, O, DD, st):
    nc = c.nc
    cv = c.sb("cv", [1, 2], I32)
    meta_sb = c.sb("meta_sb", [1, 4], F32)
    zfl = c.sb("zfl", [1, 8], I32)
    fvals = c.sb("fvals", [1, 33], I32)
    sm = c.sb("sm", [128, 256], F32)
    ones1 = c.sb("ones1", [1, 128], F32)
    onesk = c.sb("onesk", [128, 1], F32)
    ident = c.sb("ident_sb", [128, 128], BF16)
    sig_bc = c.sb("sig_bc", [128, NST], F32)
    simn = c.sb("simn", [128, DC * NST], BF16)
    row = c.sb("row", [1, 4 * TT], F32)
    bcA = c.sb("bcA", [128, TT], F32)
    bcB = c.sb("bcB", [128, TT], F32)
    c.open_stack("p01")
    w_in = c.sb("w_in", [128, DC * 2 * D], BF16, stack="p01")
    xp_w = c.sb("xp_w", [128, DC * 96], BF16, stack="p01")
    dtp_w = c.sb("dtp_w", [DTR, D], BF16, stack="p01")
    halo_xs = c.sb("halo_xs", [128, DC * 4], F32, stack="p01")
    halo_zs = c.sb("halo_zs", [128, DC * 4], F32, stack="p01")
    stash_x = c.sb("stash_x", [128, DC], F16, stack="p01")
    stash_z = c.sb("stash_z", [128, DC], F16, stack="p01")
    halo16x = c.sb("halo16x", [128, DC * 4], F16, stack="p01")
    halo16z = c.sb("halo16z", [128, DC * 4], F16, stack="p01")
    c.open_stack("p0t")
    st.update(cv=cv, meta_sb=meta_sb, fvals=fvals, sm=sm, ones1=ones1,
              onesk=onesk, ident=ident, sig_bc=sig_bc, simn=simn, w_in=w_in,
              xp_w=xp_w, dtp_w=dtp_w, halo_xs=halo_xs, halo_zs=halo_zs,
              stash_x=stash_x, stash_z=stash_z, row=row, bcA=bcA, bcB=bcB,
              halo16x=halo16x, halo16z=halo16z)

    ps0 = c.ps("ps0", [128, TT])
    ps1 = c.ps("ps1", [128, TT])
    ps2 = c.ps("ps2", [128, TT])
    ps3 = c.ps("ps3", [128, TT])
    ps4 = c.ps("ps4", [128, TT])
    ps5 = c.ps("ps5", [128, TT])
    ps6 = c.ps("ps6", [128, TT])
    ps7 = c.ps("ps7", [128, TT], BF16)
    st.update(ps0=ps0, ps1=ps1, ps2=ps2, ps3=ps3, ps4=ps4, ps5=ps5, ps6=ps6, ps7=ps7)

    @block.gpsimd
    def _(g):
        g.memset(zfl[:], 0)
        g.iota(fvals[:], [[1, 33]], base=0, channel_multiplier=0,
               allow_small_or_imprecise_dtypes=True)
        g.memset(ones1[:], 1.0)
        g.memset(onesk[:], 1.0)
        for arm in g.switch_core_id(n=128):
            pnc = int(arm) % 8
            b, h = pnc // 2, pnc % 2
            g.memset(cv[0:1, 0:1], b)
            g.memset(cv[0:1, 1:2], h)
            g.memset(meta_sb[0:1, 0:1], float(b))
            g.memset(meta_sb[0:1, 1:2], float(h))
            g.memset(meta_sb[0:1, 2:3], float(pnc))
            g.memset(meta_sb[0:1, 3:4], float(int(arm)))
            g.dma_start(out=DD["s_fl"][h:h + 1], in_=zfl[:]).then_inc(c.sem["qG"], 16)
        c.cnt["qG"] += 16
        g.wait_ge(c.sem["qG"], c.cnt["qG"])
        cc = g.collective_compute(
            "AllReduce", ALU.add, replica_groups=[list(range(8))],
            ins=[DD["bar_in"].ap().opt()], outs=[DD["bar_out"].ap().opt()])
        c.inc(cc, "bar", 1)
        g.wait_ge(c.sem["bar"], 1)

    @block.gpsimd
    def _(g):
        rB = g.alloc_register("rB")
        rH = g.alloc_register("rH")
        rS1 = g.alloc_register("rS1")
        rS2 = g.alloc_register("rS2")
        rQP = g.alloc_register("rQP")
        g.reg_load(rB, cv[0:1, 0:1])
        g.reg_load(rH, cv[0:1, 1:2])
        g.reg_mov(rQP, 0)
        st["gp_rB"], st["gp_rH"] = rB, rH
        st["gp_rS"] = [rS1, rS2]
        st["gp_rQP"] = rQP
        _RQP[0] = rQP

    hxtok = c.sb("hxtok", [3, D], F32, stack="p0t")
    hv = c.sb("hv", [1, 4], F32, stack="p0t")
    st["hv"] = hv

    @block.sync
    def _(s):
        c.w(s, "bar", 1)
        rB = s.alloc_register("syB")
        rH = s.alloc_register("syH")
        rS1 = s.alloc_register("syS1")
        rS2 = s.alloc_register("syS2")
        s.reg_load(rB, cv[0:1, 0:1])
        s.reg_load(rH, cv[0:1, 1:2])
        st["sy_rB"], st["sy_rH"] = rB, rH
        st["sy_rS"] = [rS1, rS2]
        s.dma_start(out=sm[:], in_=I["smallpm"]).then_inc(c.sem["qS"], 16)
        s.dma_start(out=ident[:], in_=I["ident"]).then_inc(c.sem["qS"], 16)
        c.cnt["qS"] += 32
        s.reg_alu(rS1, rB, D * T, ALU.mult)
        s.reg_alu(rS2, rH, TH, ALU.mult)
        s.reg_add(rS1, rS1, rS2)
        i = s.dma_start(out=DD["d_x"].ap(),
                        in_=bass.AP(I["xT"].tensor, rS1, [[T, D], [1, TH]]))
        c.inc(i, "qS", 16)
        s.reg_alu(rS1, rB, 3 * D, ALU.mult)
        i = s.dma_start(out=hxtok[:],
                        in_=bass.AP(I["xhalo"].tensor, rS1, [[D, 3], [1, D]]))
        c.inc(i, "qS", 16)
        s.reg_alu(rS1, rH, 4, ALU.mult)
        i = s.dma_start(out=hv[:], in_=bass.AP(I["hvalid"].tensor, rS1, [[4, 1], [1, 4]]))
        c.inc(i, "qS", 16)

    wstg = c.sb("wstg", [128, 2 * D], F32, stack="p0t")
    for dc in range(DC):
        @block.sync
        def _(s, dc=dc):
            if dc >= 1:
                c.w(s, "eA")
            i = s.dma_start(out=wstg[:], in_=I["w_in_T"][dc * 128:(dc + 1) * 128, :])
            c.inc(i, "qS", 16)

        @block.scalar
        def _(sc, dc=dc):
            c.w(sc, "qS")
            i = sc.activation(out=w_in[:, dc * 2 * D:(dc + 1) * 2 * D], in_=wstg[:],
                              func=AF.Copy)
            c.inc(i, "eA")

    @block.sync
    def _(s):
        c.w(s, "eA")
        i = s.dma_start(out=wstg[:, 0:DC * 96].rearrange("p (dc n) -> p dc n", n=96),
                        in_=I["xproj_T"].rearrange("(dc p) n -> p dc n", p=128))
        c.inc(i, "qS", 16)
        i = s.dma_start(out=wstg[:, DC * 96:DC * 96 + NST * DC].rearrange(
            "p (dc n) -> p dc n", n=NST),
            in_=I["A_log"].rearrange("(dc p) n -> p dc n", p=128))
        c.inc(i, "qS", 16)

    @block.scalar
    def _(sc):
        c.w(sc, "qS")
        i = sc.activation(out=xp_w[:], in_=wstg[:, 0:DC * 96], func=AF.Copy)
        c.inc(i, "eA")
        i = sc.activation(out=wstg[:, 224 * 8:224 * 8 + NST * DC],
                          in_=wstg[:, DC * 96:DC * 96 + NST * DC], func=AF.Exp)
        c.inc(i, "eA")

    @block.vector
    def _(ve):
        c.w(ve, "eA")
        i = ve.tensor_scalar(out=wstg[:, 224 * 8:224 * 8 + NST * DC],
                             in0=wstg[:, 224 * 8:224 * 8 + NST * DC],
                             scalar1=-1.0, scalar2=None, op0=ALU.mult)
        c.inc(i, "eV")

    @block.gpsimd
    def _(g):
        c.w(g, "eV")
        i = g.dma_start(out=DD["d_A"].ap().rearrange("dc p n -> p dc n"),
                        in_=wstg[:, 224 * 8:224 * 8 + NST * DC].rearrange(
                            "p (dc n) -> p dc n", n=NST))
        c.inc(i, "qG", 16)

    @block.sync
    def _(s):
        c.w(s, "qG")
        i = s.dma_start(out=wstg[0:DTR, 0:D], in_=I["dtproj_T"])
        c.inc(i, "qS", 16)

    @block.scalar
    def _(sc):
        c.w(sc, "qS")
        i = sc.activation(out=dtp_w[:], in_=wstg[0:DTR, 0:D], func=AF.Copy)
        c.inc(i, "eA")

    @block.sync
    def _(s):
        c.w(s, "eA")
        i = s.dma_start(out=wstg[:, 0:DC * NST].rearrange("p (dc n) -> p dc n", n=NST),
                        in_=I["sim"].rearrange("(dc p) n -> p dc n", p=128))
        c.inc(i, "qS", 16)
        i = s.dma_start(out=row[0:1, 1 * TT:1 * TT + NST], in_=I["gates"])
        c.inc(i, "qS", 16)

    @block.scalar
    def _(sc):
        c.w(sc, "qS")
        i = sc.activation(out=wstg[:, DC * NST:2 * DC * NST], in_=wstg[:, 0:DC * NST],
                          func=AF.Square)
        c.inc(i, "eA")
        i = sc.activation(out=row[0:1, 1 * TT + NST:1 * TT + 2 * NST], in_=row[0:1, 1 * TT:1 * TT + NST], func=AF.Sigmoid)
        c.inc(i, "eA")

    @block.tensor
    def _(te):
        c.w(te, "eA")
        for dc in range(DC):
            i = te.matmul(ps0[0:1, 0:NST], onesk[:],
                          wstg[:, DC * NST + dc * NST:DC * NST + (dc + 1) * NST],
                          start=(dc == 0), stop=(dc == DC - 1))
        c.inc(i, "eP")
        i = te.matmul(ps1[:, 0:NST], ones1[:], row[0:1, 1 * TT + NST:1 * TT + 2 * NST], start=True, stop=True)
        c.inc(i, "eP")

    @block.scalar
    def _(sc):
        c.w(sc, "eP")
        i = sc.activation(out=row[0:1, 2 * TT:2 * TT + NST], in_=ps0[0:1, 0:NST], func=AF.Sqrt)
        c.inc(i, "eA")
        i = sc.activation(out=sig_bc[:], in_=ps1[:, 0:NST], func=AF.Copy)
        c.inc(i, "eA")

    @block.vector
    def _(ve):
        c.w(ve, "eA")
        i = ve.reciprocal(out=row[0:1, 2 * TT + NST:2 * TT + 2 * NST], in_=row[0:1, 2 * TT:2 * TT + NST])
        c.inc(i, "eV")

    @block.tensor
    def _(te):
        c.w(te, "eV")
        i = te.matmul(ps2[:, 0:NST], ones1[:], row[0:1, 2 * TT + NST:2 * TT + 2 * NST], start=True, stop=True)
        c.inc(i, "eP")

    @block.scalar
    def _(sc):
        c.w(sc, "eP")
        i = sc.activation(out=wstg[:, 2 * DC * NST:2 * DC * NST + NST], in_=ps2[:, 0:NST],
                          func=AF.Copy)
        c.inc(i, "eA")

    @block.vector
    def _(ve):
        c.w(ve, "eA")
        for dc in range(DC):
            i = ve.tensor_tensor(out=simn[:, dc * NST:(dc + 1) * NST],
                                 in0=wstg[:, dc * NST:(dc + 1) * NST],
                                 in1=wstg[:, 2 * DC * NST:2 * DC * NST + NST],
                                 op=ALU.mult)
        c.inc(i, "eV")

    hx = c.sb("hx", [128, DC * 4], F32, stack="p0t")
    hrow = c.sb("hrow", [1, 16], F32, stack="p0t")
    eye4 = c.sb("eye4_sb", [4, 4], F32, stack="p0t")

    @block.sync
    def _(s):
        c.w(s, "eA")
        i = s.dma_start(out=eye4[:], in_=I["eye4"])
        c.inc(i, "qS", 16)

    @block.tensor
    def _(te):
        c.w(te, "qS"); c.w(te, "eA"); c.w(te, "eV")
        for dc in range(DC):
            i = te.matmul(ps3[:, dc * 4:dc * 4 + 3], hxtok[0:3, dc * 128:(dc + 1) * 128],
                          eye4[0:3, 0:3], start=True, stop=True)
        c.inc(i, "eP")

    @block.gpsimd
    def _(g):
        c.w(g, "eP")
        i = g.memset(hx[:], 0.0)
        c.inc(i, "eO")

    @block.vector
    def _(ve):
        c.w(ve, "eO")
        for dc in range(DC):
            i = ve.tensor_copy(out=hx[:, dc * 4:dc * 4 + 3], in_=ps3[:, dc * 4:dc * 4 + 3])
        c.inc(i, "eV")

    sq4 = c.sb("sq4", [128, DC * 4], F32, stack="p0t")
    ha = c.sb("ha", [128, DC * 4], F32, stack="p0t")
    hm = c.sb("hm", [128, DC * 4], BF16, stack="p0t")

    def halo_apply(src, dst, affine):
        @block.vector
        def _(ve):
            c.w(ve, "eA")
            for dc in range(DC):
                sl = slice(dc * 4, (dc + 1) * 4)
                i = ve.tensor_tensor(out=sq4[:, sl], in0=src[:, sl], in1=bcB[:, 0:4],
                                     op=ALU.subtract)
            c.inc(i, "eV")

        @block.gpsimd
        def _(g):
            c.w(g, "eV")
            for dc in range(DC):
                sl = slice(dc * 4, (dc + 1) * 4)
                i = g.tensor_tensor(out=dst[:, sl], in0=sq4[:, sl], in1=bcA[:, 0:4],
                                    op=ALU.mult)
            c.inc(i, "eO")

        if affine:
            @block.scalar
            def _(sc):
                c.w(sc, "eO")
                for dc in range(DC):
                    sl = slice(dc * 4, (dc + 1) * 4)
                    i = sc.activation(out=hm[:, sl], in_=dst[:, sl], func=AF.Identity,
                                      scale=sm[:, SM["ln1g"] + dc:SM["ln1g"] + dc + 1],
                                      bias=sm[:, SM["ln1b"] + dc:SM["ln1b"] + dc + 1])
                c.inc(i, "eA")

    _ln_full(c, block, st, hx, sq4, bcA, bcB, colw=4, chunkw=4)
    halo_apply(hx, ha, affine=False)
    _ln_full(c, block, st, ha, sq4, bcA, bcB, colw=4, chunkw=4)
    halo_apply(ha, ha, affine=True)


    @block.tensor
    def _(te):
        c.w(te, "eV"); c.w(te, "eA")
        for oc in range(16):
            for dc in range(DC):
                i = te.matmul(ps2[:, oc * 4:(oc + 1) * 4],
                              w_in[:, dc * 2 * D + oc * 128:dc * 2 * D + oc * 128 + 128],
                              hm[:, dc * 4:(dc + 1) * 4],
                              start=(dc == 0), stop=(dc == DC - 1))
        c.inc(i, "eP")
        c.w(te, "qS")
        i = te.matmul(ps4[:, 0:4], ones1[:], hv[:], start=True, stop=True)
        c.inc(i, "eP")

    @block.scalar
    def _(sc):
        c.w(sc, "eP")
        i = sc.activation(out=sq4[:, 0:4], in_=ps4[:, 0:4], func=AF.Copy)
        c.inc(i, "eA")

    @block.vector
    def _(ve):
        c.w(ve, "eP"); c.w(ve, "eA")
        for oc in range(16):
            dst = halo_xs if oc < 8 else halo_zs
            dcx = oc % 8
            i = ve.tensor_tensor(out=dst[:, dcx * 4:dcx * 4 + 4],
                                 in0=ps2[:, oc * 4:(oc + 1) * 4],
                                 in1=sq4[:, 0:4], op=ALU.mult)
        c.inc(i, "eV")

    if "d_halo" in _DEBUG_TAPS:
        @block.gpsimd
        def _(g):
            c.w(g, "eV"); c.w(g, "eA"); c.w(g, "qS")
            g.dma_start(out=DD["d_halo"].ap()[0], in_=hx[:]).then_inc(c.sem["qG"], 16)
            g.dma_start(out=DD["d_halo"].ap()[1], in_=ha[:]).then_inc(c.sem["qG"], 16)
            g.dma_start(out=DD["d_halo"].ap()[2], in_=halo_xs[:]).then_inc(c.sem["qG"], 16)
            i = g.dma_start(out=DD["d_halo"].ap()[3][:, 0:4], in_=sq4[:, 0:4])
            c.cnt["qG"] += 48
            c.inc(i, "qG", 16)

    @block.vector
    def _(ve):
        c.w(ve, "eV")
        ve.tensor_copy(out=halo16x[:], in_=halo_xs[:])
        ve.tensor_copy(out=halo16z[:], in_=halo_zs[:])
        ve.tensor_copy(out=stash_x[:],
                       in_=halo16x[:].rearrange("p (dc t) -> p dc t", t=4)[:, :, 0:1].rearrange("p dc o -> p (dc o)"))
        i = ve.tensor_copy(out=stash_z[:],
                           in_=halo16z[:].rearrange("p (dc t) -> p dc t", t=4)[:, :, 0:1].rearrange("p dc o -> p (dc o)"))
        c.inc(i, "eV")



def _ln_full(c, block, st, src, sq_dst, bcA, bcB, n_dc=DC, colw=TT, chunkw=None):
    row, onesk, ones1 = st["row"], st["onesk"], st["ones1"]
    ps0, ps1, ps4, ps5 = st["ps0"], st["ps1"], st["ps4"], st["ps5"]
    nden = 1.0 / (n_dc * 128)
    cw = chunkw or colw

    @block.scalar
    def _(sc):
        c.w(sc, "eV"); c.w(sc, "qS"); c.w(sc, "eP"); c.w(sc, "eO")
        for dc in range(n_dc):
            i = sc.activation(out=sq_dst[:, dc * cw:dc * cw + colw],
                              in_=src[:, dc * cw:dc * cw + colw], func=AF.Square)
        c.inc(i, "eA")

    @block.tensor
    def _(te):
        c.w(te, "eA"); c.w(te, "eV")
        for dc in range(n_dc):
            i = te.matmul(ps0[0:1, 0:colw], onesk[:],
                          src[:, dc * cw:dc * cw + colw],
                          start=(dc == 0), stop=(dc == n_dc - 1))
        c.inc(i, "eP")
        for dc in range(n_dc):
            i = te.matmul(ps1[0:1, 0:colw], onesk[:],
                          sq_dst[:, dc * cw:dc * cw + colw],
                          start=(dc == 0), stop=(dc == n_dc - 1))
        c.inc(i, "eP")

    @block.vector
    def _(ve):
        c.w(ve, "eP")
        ve.tensor_scalar(out=row[0:1, 2 * TT:2 * TT + colw], in0=ps0[0:1, 0:colw], scalar1=nden,
                         scalar2=None, op0=ALU.mult)
        ve.tensor_scalar(out=row[0:1, 3 * TT:3 * TT + colw], in0=ps1[0:1, 0:colw], scalar1=nden,
                         scalar2=None, op0=ALU.mult)
        ve.tensor_tensor(out=row[0:1, 1 * TT:1 * TT + colw], in0=row[0:1, 2 * TT:2 * TT + colw],
                         in1=row[0:1, 2 * TT:2 * TT + colw], op=ALU.mult)
        ve.tensor_tensor(out=row[0:1, 3 * TT:3 * TT + colw], in0=row[0:1, 3 * TT:3 * TT + colw],
                         in1=row[0:1, 1 * TT:1 * TT + colw], op=ALU.subtract)
        i = ve.tensor_scalar(out=row[0:1, 3 * TT:3 * TT + colw], in0=row[0:1, 3 * TT:3 * TT + colw], scalar1=EPS,
                             scalar2=None, op0=ALU.add)
        c.inc(i, "eV")

    @block.scalar
    def _(sc):
        c.w(sc, "eV")
        i = sc.activation(out=row[0:1, 1 * TT:1 * TT + colw], in_=row[0:1, 3 * TT:3 * TT + colw], func=AF.Sqrt)
        c.inc(i, "eA")

    @block.vector
    def _(ve):
        c.w(ve, "eA")
        i = ve.reciprocal(out=row[0:1, 0 * TT:0 * TT + colw], in_=row[0:1, 1 * TT:1 * TT + colw])
        c.inc(i, "eV")

    @block.tensor
    def _(te):
        c.w(te, "eV")
        te.matmul(ps4[:, 0:colw], ones1[:], row[0:1, 0 * TT:0 * TT + colw], start=True, stop=True)
        i = te.matmul(ps5[:, 0:colw], ones1[:], row[0:1, 2 * TT:2 * TT + colw], start=True, stop=True)
        c.inc(i, "eP")

    @block.scalar
    def _(sc):
        c.w(sc, "eP")
        sc.activation(out=bcA[:, 0:colw], in_=ps4[:, 0:colw], func=AF.Copy)
        i = sc.activation(out=bcB[:, 0:colw], in_=ps5[:, 0:colw], func=AF.Copy)
        c.inc(i, "eA")


def emit_p1(c, block, I, O, DD, st):
    nc = c.nc
    sm, w_in, xp_w, dtp_w = st["sm"], st["w_in"], st["xp_w"], st["dtp_w"]
    ps0, ps1, ps2, ps3 = st["ps0"], st["ps1"], st["ps2"], st["ps3"]
    ps4, ps5, ps6 = st["ps4"], st["ps5"], st["ps6"]
    halo_xs, halo_zs = st["halo_xs"], st["halo_zs"]
    stash_x, stash_z = st["stash_x"], st["stash_z"]
    fvals = st["fvals"]

    c.open_stack("p1")
    xin = c.sb("xin", [128, DC * TT], F32, stack="p1")
    a_sb = c.sb("a_sb", [128, DC * TT], F32, stack="p1")
    m_sb = c.sb("m_sb", [128, DC * TT], BF16, stack="p1")
    xs2 = [c.sb("xs2_0", [128, DC * TT], F16, stack="p1"),
           c.sb("xs2_1", [128, DC * TT], F16, stack="p1")]
    zs2 = [c.sb("zs2_0", [128, DC * TT], F16, stack="p1"),
           c.sb("zs2_1", [128, DC * TT], F16, stack="p1")]
    cacc = c.sb("cacc", [128, TT], F16, stack="p1")
    xse = [c.sb("xse0", [128, 516], F16, stack="p1"),
           c.sb("xse1", [128, 516], F16, stack="p1")]
    u_sb = c.sb("u_sb", [128, DC * TT], F32, stack="p1")
    ubf = c.sb("ubf", [128, DC * TT], BF16, stack="p1")
    udp16 = c.sb("udp16", [128, DC * TT], F16, stack="p1")
    zcb = c.sb("zcb", [128, DC * TT], BF16, stack="p1")
    del16 = c.sb("del16", [128, DC * TT], F16, stack="p1")
    w16 = c.sb("w16", [128, DC * TT], F16, stack="p1")
    dtb_sb = c.sb("dtb_sb", [DTR, TT], BF16, stack="p1")
    bcq = c.sb("bcq", [32, TT], F16, stack="p1")
    spdw = c.sb("spdw", [128, DC * TT], F32, stack="p1")
    bcA, bcB = st["bcA"], st["bcB"]
    st["xs2"], st["zs2"] = xs2, zs2



    def conv_silu(block, jj, src2, stash, halo, tapcol, biascol, out_act, out_dtype_note):
        pass

    for j in range(NTILE + 1):
        wallj = c.snap()
        if _DEBUG_TAPS:
            for sec in [block.sync, block.gpsimd, block.scalar, block.vector, block.tensor]:
                @sec
                def _(e):
                    c.wall(e)

        if j < NTILE:
            @block.sync
            def _(s, j=j):
                c.wall(s, wallj)
                ap = bass.AP(DD["d_x"], j * TT, [[TH, 128], [128 * TH, DC], [1, TT]])
                i = s.dma_start(out=xin[:].rearrange("p (dc t) -> p dc t", t=TT), in_=ap)
                c.inc(i, "qS", 16)

            _ln_full(c, block, st, xin, a_sb, bcA, bcB)

            @block.vector
            def _(ve, j=j):
                c.w(ve, "eA"); c.w(ve, "qS")
                for dc in range(DC):
                    sl = slice(dc * TT, (dc + 1) * TT)
                    ve.tensor_tensor(out=a_sb[:, sl], in0=xin[:, sl], in1=bcB[:], op=ALU.subtract)
                    i = ve.tensor_tensor(out=a_sb[:, sl], in0=a_sb[:, sl], in1=bcA[:], op=ALU.mult)
                c.inc(i, "eV")

            @block.gpsimd
            def _(g, j=j):
                c.w(g, "eV")
                ap = bass.AP(DD["d_a"], j * D * TT, [[TT, 128], [128 * TT, DC], [1, TT]])
                i = g.dma_start(out=ap, in_=a_sb[:].rearrange("p (dc t) -> p dc t", t=TT))
                c.inc(i, "qG", 16)


            _ln_full(c, block, st, a_sb, u_sb, bcA, bcB)

            @block.vector
            def _(ve, j=j):
                c.w(ve, "eA")
                for dc in range(DC):
                    sl = slice(dc * TT, (dc + 1) * TT)
                    ve.tensor_tensor(out=xin[:, sl], in0=a_sb[:, sl], in1=bcB[:], op=ALU.subtract)
                    i = ve.tensor_tensor(out=xin[:, sl], in0=xin[:, sl], in1=bcA[:], op=ALU.mult)
                c.inc(i, "eV")

            if "d_xin" in _DEBUG_TAPS:
                @block.gpsimd
                def _(g, j=j):
                    c.w(g, "eV")
                    apx = bass.AP(DD["d_xin"], j * D * TT, [[TT, 128], [128 * TT, DC], [1, TT]])
                    i = g.dma_start(out=apx, in_=xin[:].rearrange("p (dc t) -> p dc t", t=TT))
                    c.inc(i, "qG", 16)

            @block.scalar
            def _(sc, j=j):
                c.w(sc, "eV"); c.w(sc, "qG")
                for dc in range(DC):
                    sl = slice(dc * TT, (dc + 1) * TT)
                    i = sc.activation(out=m_sb[:, sl], in_=xin[:, sl], func=AF.Identity,
                                      scale=sm[:, SM["ln1g"] + dc:SM["ln1g"] + dc + 1],
                                      bias=sm[:, SM["ln1b"] + dc:SM["ln1b"] + dc + 1])
                c.inc(i, "eA")

            if "d_m" in _DEBUG_TAPS:
                @block.gpsimd
                def _(g, j=j):
                    c.w(g, "eA")
                    apm = bass.AP(DD["d_m"], j * D * TT, [[TT, 128], [128 * TT, DC], [1, TT]])
                    i = g.dma_start(out=apm, in_=m_sb[:].rearrange("p (dc t) -> p dc t", t=TT))
                    c.inc(i, "qG", 16)

            par = j % 2
            for oc in range(16):
                pb = ps2 if oc % 2 == 0 else ps3
                @block.tensor
                def _(te, oc=oc, pb=pb):
                    c.w(te, "eV"); c.w(te, "eA")
                    for dc in range(DC):
                        i = te.matmul(pb[:],
                                      w_in[:, dc * 2 * D + oc * 128:dc * 2 * D + (oc + 1) * 128],
                                      m_sb[:, dc * TT:(dc + 1) * TT],
                                      start=(dc == 0), stop=(dc == DC - 1))
                    c.inc(i, "eP")

                @block.scalar
                def _(sc, oc=oc, pb=pb, par=par):
                    c.w(sc, "eP")
                    dst = xs2[par] if oc < 8 else zs2[par]
                    dcx = oc % 8
                    i = sc.activation(out=dst[:, dcx * TT:(dcx + 1) * TT], in_=pb[:],
                                      func=AF.Copy)
                    c.inc(i, "eA")

        if j >= 1:
            jj = j - 1
            parc = jj % 2
            for cvi, (src2, stash, halo, tapc, biasc) in enumerate([
                (xs2, stash_x, st["halo16x"], SM["cwx"], SM["cxb"]),
                (zs2, stash_z, st["halo16z"], SM["cwz"], SM["czb"])]):
                cur = src2[parc]
                nxt = src2[(jj + 1) % 2]
                for dc in range(DC):
                    pe = dc % 2
                    @block.gpsimd
                    def _(g, dc=dc, pe=pe, cur=cur, nxt=nxt, stash=stash, halo=halo, jj=jj):
                        c.w(g, "eA")
                        if dc >= 2 or cvi > 0:
                            c.w(g, "eV")
                        g.tensor_copy(out=xse[pe][:, 0:1], in_=stash[:, dc:dc + 1])
                        g.tensor_copy(out=xse[pe][:, 1:513],
                                      in_=cur[:, dc * TT:(dc + 1) * TT])
                        if jj < NTILE - 1:
                            n01 = nxt[:, dc * TT:dc * TT + 2]
                        else:
                            n01 = halo[:, dc * 4 + 1:dc * 4 + 3]
                        i = g.tensor_copy(out=xse[pe][:, 513:515], in_=n01)
                        c.inc(i, "eO")

                    @block.vector
                    def _(ve, dc=dc, pe=pe, tapc=tapc):
                        c.w(ve, "eO")
                        w0 = sm[:, tapc + 0 * 8 + dc:tapc + 0 * 8 + dc + 1]
                        w1 = sm[:, tapc + 1 * 8 + dc:tapc + 1 * 8 + dc + 1]
                        w2 = sm[:, tapc + 2 * 8 + dc:tapc + 2 * 8 + dc + 1]
                        w3 = sm[:, tapc + 3 * 8 + dc:tapc + 3 * 8 + dc + 1]
                        ve.tensor_scalar(out=cacc[:], in0=xse[pe][:, 0:TT],
                                         scalar1=w0, scalar2=None, op0=ALU.mult)
                        ve.scalar_tensor_tensor(out=cacc[:], in0=xse[pe][:, 1:TT + 1],
                                                scalar=w1, in1=cacc[:],
                                                op0=ALU.mult, op1=ALU.add)
                        ve.scalar_tensor_tensor(out=cacc[:], in0=xse[pe][:, 2:TT + 2],
                                                scalar=w2, in1=cacc[:],
                                                op0=ALU.mult, op1=ALU.add)
                        i = ve.scalar_tensor_tensor(out=cacc[:], in0=xse[pe][:, 3:TT + 3],
                                                    scalar=w3, in1=cacc[:],
                                                    op0=ALU.mult, op1=ALU.add)
                        c.inc(i, "eV")

                    @block.scalar
                    def _(sc, dc=dc, cvi=cvi, biasc=biasc, jj=jj):
                        c.w(sc, "eV")
                        cs = slice(dc * TT, (dc + 1) * TT)
                        if cvi == 0:
                            i = sc.activation(out=u_sb[:, cs], in_=cacc[:], func=AF.Silu,
                                              bias=sm[:, biasc + dc:biasc + dc + 1])
                        else:
                            i = sc.activation(out=zcb[:, cs], in_=cacc[:], func=AF.Silu,
                                              bias=sm[:, biasc + dc:biasc + dc + 1])
                        c.inc(i, "eA")

                @block.vector
                def _(ve, cur=cur, stash=stash):
                    c.w(ve, "eA")
                    i = ve.tensor_copy(out=stash[:], in_=cur[:].rearrange(
                        "p (dc t) -> p dc t", t=TT)[:, :, TT - 1:TT].rearrange(
                        "p dc o -> p (dc o)"))
                    c.inc(i, "eV")

            @block.scalar
            def _(sc, jj=jj):
                c.w(sc, "eA")
                for dc in range(DC):
                    sl = slice(dc * TT, (dc + 1) * TT)
                    i = sc.activation(out=udp16[:, sl], in_=u_sb[:, sl], func=AF.Identity,
                                      scale=sm[:, SM["Dp"] + dc:SM["Dp"] + dc + 1])
                c.inc(i, "eA")

            @block.vector
            def _(ve, jj=jj):
                c.w(ve, "eA")
                for dc in range(DC):
                    sl = slice(dc * TT, (dc + 1) * TT)
                    i = ve.tensor_copy(out=ubf[:, sl], in_=u_sb[:, sl])
                c.inc(i, "eV")

            @block.gpsimd
            def _(g, jj=jj):
                c.w(g, "eV")
                if "d_u" in _DEBUG_TAPS:
                    apx = bass.AP(DD["d_u"], jj * D * TT, [[TT, 128], [128 * TT, DC], [1, TT]])
                    g.dma_start(out=apx, in_=u_sb[:].rearrange("p (dc t) -> p dc t", t=TT)
                                ).then_inc(c.sem["qG"], 16)
                    c.cnt["qG"] += 16
                if "d_xs" in _DEBUG_TAPS:
                    apx = bass.AP(DD["d_xs"], jj * D * TT, [[TT, 128], [128 * TT, DC], [1, TT]])
                    g.dma_start(out=apx, in_=xs2[jj % 2][:].rearrange("p (dc t) -> p dc t", t=TT)
                                ).then_inc(c.sem["qG"], 16)
                    c.cnt["qG"] += 16
                apu = bass.AP(DD["d_udp"], jj * D * TT, [[TT, 128], [128 * TT, DC], [1, TT]])
                g.dma_start(out=apu, in_=udp16[:].rearrange("p (dc t) -> p dc t", t=TT)
                            ).then_inc(c.sem["qG"], 16)
                c.cnt["qG"] += 16
                apz = bass.AP(DD["d_zc"], jj * D * TT, [[TT, 128], [128 * TT, DC], [1, TT]])
                i = g.dma_start(out=apz, in_=zcb[:].rearrange("p (dc t) -> p dc t", t=TT))
                c.inc(i, "qG", 16)

            @block.tensor
            def _(te, jj=jj):
                c.w(te, "eV")
                for dc in range(DC):
                    i = te.matmul(ps6[0:96, :], xp_w[:, dc * 96:(dc + 1) * 96],
                                  ubf[:, dc * TT:(dc + 1) * TT],
                                  start=(dc == 0), stop=(dc == DC - 1))
                c.inc(i, "eP")

            @block.scalar
            def _(sc):
                c.w(sc, "eP")
                i = sc.activation(out=dtb_sb[:], in_=ps6[0:DTR, :], func=AF.Copy)
                c.inc(i, "eA")

            @block.vector
            def _(ve):
                c.w(ve, "eP")
                i = ve.tensor_copy(out=bcq[:], in_=ps6[DTR:96, :])
                c.inc(i, "eV")

            @block.gpsimd
            def _(g, jj=jj):
                c.w(g, "eV")
                i = g.dma_start(out=DD["d_bcL"].ap()[jj], in_=bcq[:])
                c.inc(i, "qG", 16)

            for odc in range(DC):
                pbd = ps6 if odc % 2 == 0 else ps2
                @block.tensor
                def _(te, odc=odc, pbd=pbd):
                    c.w(te, "eA")
                    c.w(te, "eV")
                    i = te.matmul(pbd[:], dtp_w[:, odc * 128:(odc + 1) * 128],
                                  dtb_sb[:], start=True, stop=True)
                    c.inc(i, "eP")

                @block.vector
                def _(ve, odc=odc, pbd=pbd):
                    c.w(ve, "eP")
                    i = ve.tensor_scalar(out=spdw[:, odc * TT:(odc + 1) * TT], in0=pbd[:],
                                         scalar1=sm[:, SM["dtb"] + odc:SM["dtb"] + odc + 1],
                                         scalar2=None, op0=ALU.add)
                    c.inc(i, "eV")

            @block.scalar
            def _(sc):
                c.w(sc, "eV"); c.w(sc, "qG")
                i = sc.activation(out=a_sb[:], in_=spdw[:], func=AF.Abs)
                c.inc(i, "eA")
                i = sc.activation(out=xin[:], in_=a_sb[:], func=AF.Exp, scale=-1.0)
                c.inc(i, "eA")
                i = sc.activation(out=a_sb[:], in_=xin[:], func=AF.Ln, bias=1.0)
                c.inc(i, "eA")

            @block.vector
            def _(ve, jj=jj):
                c.w(ve, "eA")
                ve.tensor_scalar(out=xin[:], in0=spdw[:], scalar1=0.0,
                                 scalar2=None, op0=ALU.max)
                ve.tensor_tensor(out=spdw[:], in0=a_sb[:], in1=xin[:], op=ALU.add)
                ve.tensor_copy(out=del16[:], in_=spdw[:])
                ve.tensor_tensor(out=xin[:], in0=spdw[:], in1=u_sb[:], op=ALU.mult)
                i = ve.tensor_copy(out=w16[:], in_=xin[:])
                c.inc(i, "eV")

            @block.gpsimd
            def _(g, jj=jj):
                c.w(g, "eV")
                ap0 = bass.AP(DD["d_dwL"], jj * 2 * D * TT,
                              [[TT, 128], [128 * TT, DC], [1, TT]])
                g.dma_start(out=ap0, in_=del16[:].rearrange("p (dc t) -> p dc t", t=TT)
                            ).then_inc(c.sem["qG"], 16)
                c.cnt["qG"] += 16
                ap1 = bass.AP(DD["d_dwL"], jj * 2 * D * TT + D * TT,
                              [[TT, 128], [128 * TT, DC], [1, TT]])
                i = g.dma_start(out=ap1, in_=w16[:].rearrange("p (dc t) -> p dc t", t=TT))
                c.inc(i, "qG", 16)


def emit_p1_exchange(c, block, I, O, DD, st):
    fvals = st["fvals"]
    rH = st["gp_rH"]
    rS1, rS2 = st["gp_rS"]

    @block.gpsimd
    def _(g):
        c.w(g, "qG")
        g.reg_alu(rS1, rH, NTILE * 2 * D * TT, ALU.mult)
        g.dma_start(out=bass.AP(DD["s_dw"], rS1, [[1, NTILE * 2 * D * TT]]),
                    in_=DD["d_dwL"].ap().rearrange("j s d t -> (j s d t)")
                    ).then_inc(c.sem["qG"], 16)
        c.cnt["qG"] += 16
        g.reg_alu(rS2, rH, NTILE * 32 * TT, ALU.mult)
        g.dma_start(out=bass.AP(DD["s_bc"], rS2, [[1, NTILE * 32 * TT]]),
                    in_=DD["d_bcL"].ap().rearrange("j r t -> (j r t)")
                    ).then_inc(c.sem["qG"], 16)
        c.cnt["qG"] += 16
        g.wait_ge(c.sem["qG"], c.cnt["qG"])
        g.reg_alu(rS2, rH, 8, ALU.mult)
        g.reg_add(rS2, rS2, FDW)
        i = g.dma_start(out=bass.AP(DD["s_fl"], rS2, [[1, 1], [1, 1]]),
                        in_=fvals[0:1, 1:2])
        c.inc(i, "qG", 16)
        g.wait_ge(c.sem["qG"], c.cnt["qG"])


def emit_poll(c, block, DD, fltile, specs, tag):
    @block.gpsimd
    def _(g):
        for k, (par, slot, target) in enumerate(specs):
            with g.register(f"pf_{tag}{k}") as flr, g.register(f"px_{tag}{k}") as cc2:
                g.reg_mov(flr, 0)

                def cond(flr=flr, cc2=cc2, target=target):
                    g.reg_alu(cc2, flr, target, ALU.is_lt)
                    return cc2

                with g.While(cond):
                    g.reg_add(_RQP[0], _RQP[0], 16)
                    g.dma_start(out=fltile[0:1, k:k + 1],
                                in_=DD["s_fl"][par:par + 1, slot:slot + 1]
                                ).then_inc(c.sem["qP"], 16)
                    g.wait_ge(c.sem["qP"], _RQP[0])
                    g.reg_load(flr, fltile[0:1, k:k + 1])
        i = g.memset(fltile[0:1, 0:1], 0)
        c.inc(i, "eO")


_RQP = [None]


def emit_p2(c, block, I, O, DD, st):
    nc = c.nc
    sm, ones1 = st["sm"], st["ones1"]
    ps4, ps5 = st["ps4"], st["ps5"]
    fvals = st["fvals"]
    rH = st["gp_rH"]
    rS1, rS2 = st["gp_rS"]

    c.open_stack("p2")
    dsc = [c.sb(f"dsc{i}", [128, 4 * TT], F16, stack="p2") for i in range(2)]
    wsc = [c.sb(f"wsc{i}", [128, 4 * TT], F16, stack="p2") for i in range(2)]
    Bb = [c.sb(f"Bb{i}", [128, NST * TT], F16, stack="p2") for i in range(2)]
    Cb = [c.sb(f"Cb{i}", [128, NST * TT], F16, stack="p2") for i in range(2)]
    dAa = [c.sb(f"dAa{i}", [128, NST * TT], F16, stack="p2") for i in range(2)]
    dBu = [c.sb(f"dBu{i}", [128, NST * TT], F16, stack="p2") for i in range(2)]
    har = [c.sb(f"har{i}", [128, NST * TT], F16, stack="p2") for i in range(2)]
    yl = [c.sb(f"yl{i}", [128, 4 * TT], F16, stack="p2") for i in range(2)]
    st_sc = c.sb("st_sc", [128, 4 * NST], F32, stack="p2")
    Ascan = c.sb("Ascan", [128, 4 * NST], F32, stack="p2")
    flp2 = c.sb("flp2", [1, 2], I32, stack="p2")

    emit_poll(c, block, DD, flp2, [(0, FDW, 1), (1, FDW, 1)], "p2")

    @block.sync
    def _(s):
        c.w(s, "eO")
        rB2 = st["sy_rS"][0]
        s.reg_load(rB2, st["cv"][0:1, 1:2])
        s.reg_alu(rB2, rB2, 512 * TT, ALU.mult)
        ap = bass.AP(DD["s_dw"], rB2,
                     [[NTILE * 2 * D * TT, 2], [2 * D * TT, NTILE], [D * TT, 2],
                      [1, 512 * TT]])
        dst = DD["d_dwS"].ap().rearrange("tc s r t -> tc s (r t)").rearrange(
            "(par j) s f -> par j s f", par=2)
        i = s.dma_start(out=dst, in_=ap)
        c.inc(i, "qS", 16)
        rB3 = st["sy_rS"][1]
        s.reg_load(rB3, st["cv"][0:1, 1:2])
        s.reg_alu(rB3, rB3, 4 * 128 * NST, ALU.mult)
        ap2 = bass.AP(DD["d_A"], rB3, [[1, 4 * 128 * NST]])
        i = s.dma_start(out=DD["d_AS"].ap().rearrange("a p n -> (a p n)").unsqueeze(0),
                        in_=ap2.unsqueeze(0))
        c.inc(i, "qS", 16)
        s.wait_ge(c.sem["qS"], c.cnt["qS"])
        i = s.dma_start(out=Ascan[:].rearrange("p (a n) -> p a n", n=NST),
                        in_=DD["d_AS"].ap().rearrange("a p n -> p a n"))
        c.inc(i, "qS", 16)

    @block.gpsimd
    def _(g):
        c.w(g, "qS")
        i = g.memset(st_sc[:], 0.0)
        c.inc(i, "eO")

    M_act, M_dve, M_dbu, M_load = {}, {}, {}, {}

    def emit_loads(tc):
        par, j = tc // 8, tc % 8
        p = tc % 2

        @block.sync
        def _(s, tc=tc, par=par, j=j, p=p):
            if tc >= 2:
                so = (tc - 2) * 4 + 3
                c.w(s, "eA", M_act[so])
                c.w(s, "eO", M_dbu[so])
                c.w(s, "eV", M_dve[so])
            ap = bass.AP(DD["d_dwS"], tc * 2 * 512 * TT,
                         [[TT, 128], [128 * TT, 4], [1, TT]])
            s.dma_start(out=dsc[p][:].rearrange("p2 (a t) -> p2 a t", t=TT), in_=ap
                        ).then_inc(c.sem["qS"], 16)
            ap = bass.AP(DD["d_dwS"], tc * 2 * 512 * TT + 512 * TT,
                         [[TT, 128], [128 * TT, 4], [1, TT]])
            s.dma_start(out=wsc[p][:].rearrange("p2 (a t) -> p2 a t", t=TT), in_=ap
                        ).then_inc(c.sem["qS"], 16)
            c.cnt["qS"] += 32
            for n in range(32):
                dst = Bb[p] if n < NST else Cb[p]
                src = bass.AP(DD["s_bc"], ((par * NTILE + j) * 32 + n) * TT,
                              [[0, 128], [1, TT]])
                i = s.dma_start(out=dst[:, (n % NST) * TT:(n % NST + 1) * TT], in_=src)
                c.inc(i, "qS", 16)
            M_load[tc] = c.cnt["qS"]

    emit_loads(0)
    emit_loads(1)

    @block.gpsimd
    def _(g):
        g.wait_ge(c.sem["qS"], M_load[0])
        for n in range(NST):
            i = g.tensor_tensor(out=dBu[0][:, n * TT:(n + 1) * TT],
                                in0=wsc[0][:, 0:TT],
                                in1=Bb[0][:, n * TT:(n + 1) * TT], op=ALU.mult)
        c.inc(i, "eO")
    M_dbu[0] = c.cnt["eO"]

    def emit_exps(s_):
        tc, dcl = s_ // 4, s_ % 4
        p, q = tc % 2, s_ % 2

        @block.scalar
        def _(sc, s_=s_, tc=tc, dcl=dcl, p=p, q=q):
            sc.wait_ge(c.sem["qS"], M_load[tc])
            if s_ >= 2:
                c.w(sc, "eV", M_dve[s_ - 2])
            for n in range(NST):
                i = sc.activation(out=dAa[q][:, n * TT:(n + 1) * TT],
                                  in_=dsc[p][:, dcl * TT:(dcl + 1) * TT], func=AF.Exp,
                                  scale=Ascan[:, dcl * NST + n:dcl * NST + n + 1])
            c.inc(i, "eA")
        M_act[s_] = c.cnt["eA"]

    emit_exps(0)
    emit_exps(1)
    for s_ in range(64):
        tc, dcl = s_ // 4, s_ % 4
        p, q = tc % 2, s_ % 2

        if s_ + 1 < 64:
            s1 = s_ + 1
            tc1, dcl1 = s1 // 4, s1 % 4
            p1, q1 = tc1 % 2, s1 % 2

            @block.gpsimd
            def _(g, s1=s1, tc1=tc1, dcl1=dcl1, p1=p1, q1=q1):
                g.wait_ge(c.sem["qS"], M_load[tc1])
                if s1 >= 2:
                    c.w(g, "eV", M_dve[s1 - 2])
                for n in range(NST):
                    i = g.tensor_tensor(out=dBu[q1][:, n * TT:(n + 1) * TT],
                                        in0=wsc[p1][:, dcl1 * TT:(dcl1 + 1) * TT],
                                        in1=Bb[p1][:, n * TT:(n + 1) * TT], op=ALU.mult)
                c.inc(i, "eO")
            M_dbu[s1] = c.cnt["eO"]

        @block.vector
        def _(ve, s_=s_, tc=tc, dcl=dcl, p=p, q=q):
            ve.wait_ge(c.sem["qS"], M_load[tc])
            c.w(ve, "eA", M_act[s_])
            c.w(ve, "eO", M_dbu[s_])
            for n in range(NST):
                ve.tensor_tensor_scan(out=har[q][:, n * TT:(n + 1) * TT],
                                      data0=dAa[q][:, n * TT:(n + 1) * TT],
                                      data1=dBu[q][:, n * TT:(n + 1) * TT],
                                      initial=st_sc[:, dcl * NST + n:dcl * NST + n + 1],
                                      op0=ALU.mult, op1=ALU.add)
            ve.tensor_tensor(out=dBu[q][:], in0=har[q][:], in1=Cb[p][:], op=ALU.mult)
            ve.tensor_tensor(out=dBu[q][:, 0:8 * TT], in0=dBu[q][:, 0:8 * TT],
                             in1=dBu[q][:, 8 * TT:16 * TT], op=ALU.add)
            ve.tensor_tensor(out=dBu[q][:, 0:4 * TT], in0=dBu[q][:, 0:4 * TT],
                             in1=dBu[q][:, 4 * TT:8 * TT], op=ALU.add)
            ve.tensor_tensor(out=dBu[q][:, 0:2 * TT], in0=dBu[q][:, 0:2 * TT],
                             in1=dBu[q][:, 2 * TT:4 * TT], op=ALU.add)
            i = ve.tensor_tensor(out=yl[p][:, dcl * TT:(dcl + 1) * TT],
                                 in0=dBu[q][:, 0:TT], in1=dBu[q][:, TT:2 * TT],
                                 op=ALU.add)
            c.inc(i, "eV")
        M_dve[s_] = c.cnt["eV"]

        if s_ + 2 < 64:
            emit_exps(s_ + 2)

        @block.scalar
        def _(sc, s_=s_, dcl=dcl, q=q):
            c.w(sc, "eV", M_dve[s_])
            for n in range(NST):
                i = sc.activation(out=st_sc[:, dcl * NST + n:dcl * NST + n + 1],
                                  in_=har[q][:, (n + 1) * TT - 1:(n + 1) * TT],
                                  func=AF.Copy)
            c.inc(i, "eA")

        if dcl == 3:
            @block.sync
            def _(s, tc=tc, s_=s_, p=p):
                c.w(s, "eV", M_dve[s_])
                ap = bass.AP(DD["d_yL"], tc * 4 * 128 * TT,
                             [[TT, 128], [128 * TT, 4], [1, TT]])
                i = s.dma_start(out=ap, in_=yl[p][:].rearrange("p2 (a t) -> p2 a t", t=TT))
                c.inc(i, "qS", 16)
            if tc + 2 <= 15:
                emit_loads(tc + 2)

    @block.gpsimd
    def _(g):
        c.w(g, "qS")
        c.w(g, "qG")
        g.reg_alu(rS1, rH, 16 * 4 * 128 * TT, ALU.mult)
        g.dma_start(out=bass.AP(DD["s_y"], rS1, [[1, 16 * 4 * 128 * TT]]),
                    in_=DD["d_yL"].ap().rearrange("a b p t -> (a b p t)").unsqueeze(0)
                    ).then_inc(c.sem["qG"], 16)
        c.cnt["qG"] += 16
        g.wait_ge(c.sem["qG"], c.cnt["qG"])
        g.reg_alu(rS2, rH, 8, ALU.mult)
        g.reg_add(rS2, rS2, FY)
        i = g.dma_start(out=bass.AP(DD["s_fl"], rS2, [[1, 1], [1, 1]]),
                        in_=fvals[0:1, 1:2])
        c.inc(i, "qG", 16)
        g.wait_ge(c.sem["qG"], c.cnt["qG"])


def _cast_weights(c, block, I, src_name, dst_sb, nchunk, width, stg, rows=128):
    for ch in range(nchunk):
        @block.sync
        def _(s, ch=ch):
            c.w(s, "eA")
            i = s.dma_start(out=stg[0:rows, 0:width],
                            in_=I[src_name][ch * rows:ch * rows + rows, 0:width])
            c.inc(i, "qS", 16)

        @block.scalar
        def _(sc, ch=ch):
            c.w(sc, "qS")
            i = sc.activation(out=dst_sb[0:rows, ch * width:(ch + 1) * width],
                              in_=stg[0:rows, 0:width], func=AF.Copy)
            c.inc(i, "eA")


def emit_p3(c, block, I, O, DD, st):
    nc = c.nc
    sm, ones1, onesk = st["sm"], st["ones1"], st["onesk"]
    ps0, ps1, ps2, ps3 = st["ps0"], st["ps1"], st["ps2"], st["ps3"]
    ps4, ps5, ps6 = st["ps4"], st["ps5"], st["ps6"]
    bcA, bcB = st["bcA"], st["bcB"]
    fvals = st["fvals"]
    rH = st["gp_rH"]
    rS1, rS2 = st["gp_rS"]

    c.open_stack("p3A")
    w_out = c.sb("w_out", [128, 16 * D], BF16, stack="p3A")
    stgA = c.sb("stgA", [128, D], F32, stack="p3A")
    y_sb = [c.sb(f"y_sb{i}", [128, DC * TT], F16, stack="p3A") for i in range(2)]
    udp_sb = [c.sb(f"udp_sb{i}", [128, DC * TT], F16, stack="p3A") for i in range(2)]
    zc_sb = [c.sb(f"zc_sb{i}", [128, DC * TT], BF16, stack="p3A") for i in range(2)]
    y1b = [c.sb(f"y1b{i}", [128, DC * TT], BF16, stack="p3A") for i in range(2)]
    a_sb2 = [c.sb(f"a_sb2{i}", [128, DC * TT], F32, stack="p3A") for i in range(2)]
    a2_sb = [c.sb(f"a2_sb{i}", [128, DC * TT], F32, stack="p3A") for i in range(2)]
    sqA = c.sb("sqA", [128, DC * TT], F32, stack="p3A")
    h2_sb = c.sb("h2_sb", [128, DC * TT], BF16, stack="p3A")
    flp3 = c.sb("flp3", [1, 2], I32, stack="p3A")

    emit_poll(c, block, DD, flp3, [(0, FY, 1), (1, FY, 1)], "p3y")

    @block.sync
    def _(s):
        c.w(s, "eO")
        rB2 = st["sy_rS"][0]
        s.reg_load(rB2, st["cv"][0:1, 1:2])
        s.reg_alu(rB2, rB2, NTILE * 4 * 128 * TT, ALU.mult)
        for p in range(2):
            if p == 1:
                s.reg_add(rB2, rB2, 16 * 4 * 128 * TT)
            ap = bass.AP(DD["s_y"], rB2, [[NTILE * 4 * 128 * TT, 1], [1, NTILE * 4 * 128 * TT]])
            dst = bass.AP(DD["d_yM"], p * NTILE * 4 * 128 * TT,
                          [[NTILE * 4 * 128 * TT, 1], [1, NTILE * 4 * 128 * TT]])
            i = s.dma_start(out=dst, in_=ap)
            c.inc(i, "qS", 16)

    _cast_weights(c, block, I, "outT", w_out, 16, D, stgA)

    MA_ld, MA_y1f, MA_y1b, MA_mm, MA_add = {}, {}, {}, {}, {}

    def emit_loadsA(j):
        @block.sync
        def _(s, j=j):
            if j >= 2:
                c.w(s, "eP", MA_mm[(j - 2, 7)])
                c.w(s, "eV", MA_add[(j - 2, 7)])
            b = j % 2
            for p in range(2):
                ap = bass.AP(DD["d_yM"], p * NTILE * 4 * 128 * TT + j * 4 * 128 * TT,
                             [[TT, 128], [128 * TT, 4], [1, TT]])
                s.dma_start(out=y_sb[b][:, p * 4 * TT:(p + 1) * 4 * TT].rearrange(
                    "p2 (a t) -> p2 a t", t=TT), in_=ap).then_inc(c.sem["qS"], 16)
            ap = bass.AP(DD["d_udp"], j * D * TT, [[TT, 128], [128 * TT, DC], [1, TT]])
            s.dma_start(out=udp_sb[b][:].rearrange("p (a t) -> p a t", t=TT), in_=ap
                        ).then_inc(c.sem["qS"], 16)
            ap = bass.AP(DD["d_zc"], j * D * TT, [[TT, 128], [128 * TT, DC], [1, TT]])
            s.dma_start(out=zc_sb[b][:].rearrange("p (a t) -> p a t", t=TT), in_=ap
                        ).then_inc(c.sem["qS"], 16)
            ap = bass.AP(DD["d_a"], j * D * TT, [[TT, 128], [128 * TT, DC], [1, TT]])
            i = s.dma_start(out=a_sb2[b][:].rearrange("p (a t) -> p a t", t=TT), in_=ap)
            c.cnt["qS"] += 4 * 16
            c.inc(i, "qS", 16)
        MA_ld[j] = c.cnt["qS"]

    emit_loadsA(0)
    for j in range(NTILE):
        b = j % 2
        if j + 1 < NTILE:
            emit_loadsA(j + 1)

        @block.vector
        def _(ve, j=j, b=b):
            ve.wait_ge(c.sem["qS"], MA_ld[j])
            i = ve.tensor_tensor(out=y_sb[b][:], in0=y_sb[b][:], in1=udp_sb[b][:],
                                 op=ALU.add)
            c.inc(i, "eV")
        MA_y1f[j] = c.cnt["eV"]

        @block.scalar
        def _(sc, j=j, b=b):
            c.w(sc, "eV", MA_y1f[j])
            i = sc.activation(out=y1b[b][:], in_=y_sb[b][:], func=AF.Copy)
            c.inc(i, "eA")
        MA_y1b[j] = c.cnt["eA"]

        for oc in range(DC):
            pb = ps2 if oc % 2 == 0 else ps3
            @block.tensor
            def _(te, j=j, b=b, oc=oc, pb=pb):
                c.w(te, "eA", MA_y1b[j])
                if oc >= 2:
                    c.w(te, "eV", MA_add[(j, oc - 2)])
                elif j >= 1:
                    c.w(te, "eV", MA_add[(j - 1, 6 + oc)])
                for kc in range(16):
                    rhs = (y1b[b][:, kc * TT:(kc + 1) * TT] if kc < 8 else
                           zc_sb[b][:, (kc - 8) * TT:(kc - 7) * TT])
                    i = te.matmul(pb[:], w_out[:, kc * D + oc * 128:kc * D + (oc + 1) * 128],
                                  rhs, start=(kc == 0), stop=(kc == 15))
                c.inc(i, "eP")
            MA_mm[(j, oc)] = c.cnt["eP"]

            @block.vector
            def _(ve, j=j, b=b, oc=oc, pb=pb):
                c.w(ve, "eP", MA_mm[(j, oc)])
                i = ve.tensor_tensor(out=a2_sb[b][:, oc * TT:(oc + 1) * TT], in0=pb[:],
                                     in1=a_sb2[b][:, oc * TT:(oc + 1) * TT], op=ALU.add)
                c.inc(i, "eV")
            MA_add[(j, oc)] = c.cnt["eV"]

        @block.gpsimd
        def _(g, j=j, b=b):
            c.w(g, "eV", MA_add[(j, 7)])
            ap = bass.AP(DD["d_a2"], j * D * TT, [[TT, 128], [128 * TT, DC], [1, TT]])
            i = g.dma_start(out=ap, in_=a2_sb[b][:].rearrange("p (a t) -> p a t", t=TT))
            c.inc(i, "qG", 16)

        _ln_full(c, block, st, a2_sb[b], sqA, bcA, bcB)

        @block.vector
        def _(ve, j=j, b=b):
            c.w(ve, "eA")
            for dc in range(DC):
                sl = slice(dc * TT, (dc + 1) * TT)
                ve.tensor_tensor(out=sqA[:, sl], in0=a2_sb[b][:, sl], in1=bcB[:], op=ALU.subtract)
                i = ve.tensor_tensor(out=sqA[:, sl], in0=sqA[:, sl], in1=bcA[:], op=ALU.mult)
            c.inc(i, "eV")

        @block.scalar
        def _(sc, j=j):
            c.w(sc, "eV"); c.w(sc, "qG")
            for dc in range(DC):
                sl = slice(dc * TT, (dc + 1) * TT)
                i = sc.activation(out=h2_sb[:, sl], in_=sqA[:, sl], func=AF.Identity,
                                  scale=sm[:, SM["ln2g"] + dc:SM["ln2g"] + dc + 1],
                                  bias=sm[:, SM["ln2b"] + dc:SM["ln2b"] + dc + 1])
            c.inc(i, "eA")

        @block.gpsimd
        def _(g, j=j):
            c.w(g, "eA")
            ap = bass.AP(DD["d_h2"], j * D * TT, [[TT, 128], [128 * TT, DC], [1, TT]])
            i = g.dma_start(out=ap, in_=h2_sb[:].rearrange("p (a t) -> p a t", t=TT))
            c.inc(i, "qG", 16)

    phase_wall(c, block)
    c.close_stack("p3A")

    c.open_stack("p3B")
    w1 = c.sb("w1", [128, DC * MLP], BF16, stack="p3B")
    w2sb = c.sb("w2sb", [128, 32 * D], BF16, stack="p3B")
    stgB = [c.sb(f"stgB{i}", [128, TT], F32, stack="p3B") for i in range(4)]
    h2l = c.sb("h2l", [128, DC * TT], BF16, stack="p3B")
    g1_bf = [c.sb(f"g1_bf{i}", [128, 8 * TT], BF16, stack="p3B") for i in range(2)]
    x1acc = c.sb("x1acc", [128, DC * TT], F32, stack="p3B")
    a2l = c.sb("a2l", [128, DC * TT], F32, stack="p3B")

    cast_items = []
    for g in range(4):
        for ch in range(DC):
            for sub in (2 * g, 2 * g + 1):
                cast_items.append(("w1T", ch * 128, sub * TT, w1, ch * MLP + sub * TT))
    for g in range(4):
        for ch in range(g * 8, (g + 1) * 8):
            for half in range(2):
                cast_items.append(("w2T", ch * 128, half * TT, w2sb, ch * D + half * TT))
    M_cast = {}
    M_w1g, M_w2g = {}, {}
    for idx, (srcn, r0, c0, dst, d0) in enumerate(cast_items):
        @block.sync
        def _(s, idx=idx, srcn=srcn, r0=r0, c0=c0):
            if idx >= 4:
                c.w(s, "eA", M_cast[idx - 4])
            i = s.dma_start(out=stgB[idx % 4][:],
                            in_=I[srcn][r0:r0 + 128, c0:c0 + TT])
            c.inc(i, "qS", 16)

        @block.scalar
        def _(sc, idx=idx, dst=dst, d0=d0):
            c.w(sc, "qS")
            i = sc.activation(out=dst[:, d0:d0 + TT], in_=stgB[idx % 4][:], func=AF.Copy)
            c.inc(i, "eA")
        M_cast[idx] = c.cnt["eA"]
        if idx < 64 and idx % 16 == 15:
            M_w1g[idx // 16] = M_cast[idx]
        if idx >= 64 and (idx - 64) % 16 == 15:
            M_w2g[(idx - 64) // 16] = M_cast[idx]

    M_ldh, M_lda, M_comb, M_x1sp = {}, {}, {}, {}
    M_mlp1, M_gelu, M_mlp2, M_add = {}, {}, {}, {}
    psm1 = [ps0, ps1, ps2, ps3]
    psm2 = [ps4, ps5]

    def emit_mlp1(G):
        j, grp = G // 4, G % 4
        for og in range(8):
            oc = grp * 8 + og
            pb = psm1[og % 4]
            @block.tensor
            def _(te, G=G, j=j, oc=oc, og=og, pb=pb):
                te.wait_ge(c.sem["qS"], M_ldh[j])
                gate = 0
                if og >= 4:
                    gate = M_gelu[(G, og - 4)]
                elif G >= 1:
                    gate = M_gelu[(G - 1, og + 4)]
                c.w(te, "eA", max(gate, M_w1g[G % 4]))
                for dc in range(DC):
                    i = te.matmul(pb[:], w1[:, dc * MLP + oc * 128:dc * MLP + (oc + 1) * 128],
                                  h2l[:, dc * TT:(dc + 1) * TT],
                                  start=(dc == 0), stop=(dc == DC - 1))
                c.inc(i, "eP")
            M_mlp1[(G, og)] = c.cnt["eP"]

            @block.scalar
            def _(sc, G=G, oc=oc, og=og, pb=pb):
                gate = M_mlp1[(G, og)]
                if G >= 2:
                    gate = max(gate, M_mlp2[(G - 2, 7)])
                c.w(sc, "eP", gate)
                i = sc.activation(out=g1_bf[G % 2][:, og * TT:(og + 1) * TT], in_=pb[:],
                                  func=AF.Gelu,
                                  bias=sm[:, SM["b1"] + oc:SM["b1"] + oc + 1])
                c.inc(i, "eA")
            M_gelu[(G, og)] = c.cnt["eA"]

    def emit_mlp2(G):
        j = G // 4
        for oc2 in range(DC):
            pb = psm2[oc2 % 2]
            @block.tensor
            def _(te, G=G, oc2=oc2, pb=pb):
                c.w(te, "eA", max(M_gelu[(G, 7)], M_w2g[G % 4]))
                if oc2 >= 2:
                    c.w(te, "eV", M_add[(G, oc2 - 2)])
                elif G >= 1:
                    c.w(te, "eV", M_add[(G - 1, 6 + oc2)])
                for kcg in range(8):
                    i = te.matmul(pb[:], w2sb[:, (G % 4) * 8 * D + kcg * D +
                                               oc2 * 128:(G % 4) * 8 * D + kcg * D +
                                               (oc2 + 1) * 128],
                                  g1_bf[G % 2][:, kcg * TT:(kcg + 1) * TT],
                                  start=(kcg == 0), stop=(kcg == 7))
                c.inc(i, "eP")
            M_mlp2[(G, oc2)] = c.cnt["eP"]

            @block.vector
            def _(ve, G=G, j=j, oc2=oc2, pb=pb):
                c.w(ve, "eP", M_mlp2[(G, oc2)])
                if G % 4 == 0 and oc2 == 0:
                    ve.wait_ge(c.sem["qS"], M_lda[j])
                sl = slice(oc2 * TT, (oc2 + 1) * TT)
                i = ve.tensor_tensor(out=x1acc[:, sl], in0=x1acc[:, sl],
                                     in1=pb[:], op=ALU.add)
                c.inc(i, "eV")
            M_add[(G, oc2)] = c.cnt["eV"]

    def emit_loads_h(j):
        @block.sync
        def _(s, j=j):
            if j >= 1:
                c.w(s, "eP", M_mlp1[((j - 1) * 4 + 3, 7)])
            ap = bass.AP(DD["d_h2"], j * D * TT, [[TT, 128], [128 * TT, DC], [1, TT]])
            i = s.dma_start(out=h2l[:].rearrange("p (a t) -> p a t", t=TT),
                            in_=ap)
            c.inc(i, "qS", 16)
        M_ldh[j] = c.cnt["qS"]

    def emit_loads_a(j):
        @block.sync
        def _(s, j=j):
            if j >= 1:
                c.w(s, "eV", M_comb[j - 1])
                c.w(s, "qG", M_x1sp[j - 1])
            ap = bass.AP(DD["d_a2"], j * D * TT, [[TT, 128], [128 * TT, DC], [1, TT]])
            s.dma_start(out=a2l[:].rearrange("p (a t) -> p a t", t=TT), in_=ap
                        ).then_inc(c.sem["qS"], 16)
            c.cnt["qS"] += 16
            ap = bass.AP(DD["d_x"], j * TT, [[TH, 128], [128 * TH, DC], [1, TT]])
            i = s.dma_start(out=x1acc[:].rearrange("p (a t) -> p a t", t=TT), in_=ap)
            c.inc(i, "qS", 16)
        M_lda[j] = c.cnt["qS"]

    def emit_combine(j):
        @block.vector
        def _(ve, j=j):
            ve.wait_ge(c.sem["qS"], M_lda[j])
            for dc in range(DC):
                sl = slice(dc * TT, (dc + 1) * TT)
                i = ve.scalar_tensor_tensor(out=x1acc[:, sl], in0=x1acc[:, sl],
                                            scalar=sm[:, SM["b2"] + dc:SM["b2"] + dc + 1],
                                            in1=a2l[:, sl], op0=ALU.add, op1=ALU.add)
            c.inc(i, "eV")
        M_comb[j] = c.cnt["eV"]

        @block.gpsimd
        def _(g, j=j):
            c.w(g, "eV", M_comb[j])
            ap = bass.AP(DD["d_x1"], j * D * TT, [[TT, 128], [128 * TT, DC], [1, TT]])
            i = g.dma_start(out=ap, in_=x1acc[:].rearrange("p (a t) -> p a t", t=TT))
            c.inc(i, "qG", 16)
        M_x1sp[j] = c.cnt["qG"]

    emit_loads_h(0)
    emit_loads_a(0)
    for G in range(33):
        if G % 4 == 0 and 1 <= G // 4 <= 7:
            emit_loads_h(G // 4)
        if G < 32:
            emit_mlp1(G)
        if G >= 1:
            emit_mlp2(G - 1)
            if (G - 1) % 4 == 3:
                k = (G - 1) // 4
                emit_combine(k)
                if k + 1 <= 7:
                    emit_loads_a(k + 1)

    for j in range(NTILE):
        @block.sync
        def _(s, j=j):
            c.w(s, "eA"); c.w(s, "eV"); c.w(s, "qG")
            ap = bass.AP(DD["d_x1"], j * D * TT, [[TT, 128], [128 * TT, DC], [1, TT]])
            i = s.dma_start(out=x1acc[:].rearrange("p (a t) -> p a t", t=TT), in_=ap)
            c.inc(i, "qS", 16)

        _ln_full(c, block, st, x1acc, a2l, bcA, bcB)

        @block.vector
        def _(ve, j=j):
            c.w(ve, "eA"); c.w(ve, "qG")
            for dc in range(DC):
                sl = slice(dc * TT, (dc + 1) * TT)
                ve.tensor_tensor(out=a2l[:, sl], in0=x1acc[:, sl], in1=bcB[:], op=ALU.subtract)
                i = ve.tensor_tensor(out=a2l[:, sl], in0=a2l[:, sl], in1=bcA[:], op=ALU.mult)
            c.inc(i, "eV")

        @block.scalar
        def _(sc, j=j):
            c.w(sc, "eV")
            for dc in range(DC):
                sl = slice(dc * TT, (dc + 1) * TT)
                i = sc.activation(out=h2l[:, sl], in_=a2l[:, sl], func=AF.Copy)
            c.inc(i, "eA")

        @block.vector
        def _(ve, j=j):
            c.w(ve, "eA")
            row = st["row"]
            ve.tensor_scalar(out=st["row"][0:1, 0:TT], in0=row[0:1, 3 * TT:4 * TT],
                             scalar1=-EPS, scalar2=None, op0=ALU.add)
            ve.tensor_scalar(out=st["row"][0:1, 0:TT], in0=st["row"][0:1, 0:TT],
                             scalar1=1e-30, scalar2=None, op0=ALU.max)
            i = ve.reciprocal(out=st["row"][0:1, TT:2 * TT], in_=st["row"][0:1, 0:TT])
            c.inc(i, "eV")

        @block.gpsimd
        def _(g, j=j):
            c.w(g, "eV")
            i = g.tensor_scalar(out=st["row"][0:1, 0:TT], in0=st["row"][0:1, TT:2 * TT],
                                scalar1=EPS, scalar2=1.0, op0=ALU.mult, op1=ALU.add)
            c.inc(i, "eO")

        @block.scalar
        def _(sc, j=j):
            c.w(sc, "eO")
            i = sc.activation(out=st["row"][0:1, TT:2 * TT], in_=st["row"][0:1, 0:TT], func=AF.Sqrt)
            c.inc(i, "eA")

        @block.vector
        def _(ve, j=j):
            c.w(ve, "eA")
            i = ve.tensor_scalar(out=st["row"][0:1, 0:TT], in0=st["row"][0:1, TT:2 * TT],
                                 scalar1=1.0 / 32.0, scalar2=None, op0=ALU.mult)
            c.inc(i, "eV")

        @block.gpsimd
        def _(g, j=j):
            c.w(g, "eV"); c.w(g, "eA")
            ap = bass.AP(DD["d_fl"], j * D * TT, [[TT, 128], [128 * TT, DC], [1, TT]])
            g.dma_start(out=ap, in_=h2l[:].rearrange("p (a t) -> p a t", t=TT)
                        ).then_inc(c.sem["qG"], 16)
            c.cnt["qG"] += 16
            ap = bass.AP(DD["d_rn"], j * TT, [[TT, 1], [1, TT]])
            i = g.dma_start(out=ap, in_=st["row"][0:1, 0:TT])
            c.inc(i, "qG", 16)

    phase_wall(c, block)
    c.close_stack("p3B")

    c.open_stack("p3C")
    qw = c.sb("qw", [128, DC * D], BF16, stack="p3C")
    kw = c.sb("kw", [128, DC * D], BF16, stack="p3C")
    vw = c.sb("vw", [128, DC * D], BF16, stack="p3C")
    stgC = c.sb("stgC", [128, D], F32, stack="p3C")
    flc = c.sb("flc", [128, DC * TT], BF16, stack="p3C")
    rn_pm = c.sb("rn_pm2", [128, 32], F32, stack="p3C")
    slog = c.sb("slog", [128, 512], F32, stack="p3C")
    slog2 = c.sb("slog2", [128, 512], F32, stack="p3C")
    eqt = c.sb("eqt", [128, 512], F32, stack="p3C")
    zb = c.sb("zb", [128, 512], F32, stack="p3C")
    kA = c.sb("kA", [128, 512], F32, stack="p3C")
    kB = c.sb("kB", [128, 512], F32, stack="p3C")
    dif = c.sb("dif", [128, 512], F32, stack="p3C")
    keepm = c.sb("keepm", [128, 512], F32, stack="p3C")
    keep = c.sb("keep", [128, 512], F32, stack="p3C")
    gated = c.sb("gated", [128, 512], F32, stack="p3C")
    t1g = c.sb("t1g", [128, 512], F32, stack="p3C")
    a_g = c.sb("a_g", [128, 512], F32, stack="p3C")
    maskd = c.sb("maskd", [128, 512], F32, stack="p3C")
    e_g = c.sb("e_g", [128, 512], F32, stack="p3C")
    rwf = c.sb("rwf", [128, 512], F32, stack="p3C")
    rw_all = c.sb("rw_all", [128, 512], BF16, stack="p3C")
    sigrep = c.sb("sigrep", [128, 512], F32, stack="p3C")
    onesw = c.sb("onesw", [128, 32], F32, stack="p3C")
    m1 = c.sb("m1", [128, 32], F32, stack="p3C")
    m2 = c.sb("m2", [128, 32], F32, stack="p3C")
    mx = c.sb("mx", [128, 32], F32, stack="p3C")
    ssum = c.sb("ssum", [128, 32], F32, stack="p3C")
    ssum2 = c.sb("ssum2", [128, 32], F32, stack="p3C")
    rs_g = c.sb("rs_g", [128, 32], F32, stack="p3C")
    activ = c.sb("activ", [128, 32], F32, stack="p3C")
    rtr = c.sb("rtr", [32, 128], BF16, stack="p3C")
    rwb = c.sb("rwb", [64, NST * TT], BF16, stack="p3C")
    accq = c.sb("accq", [64, TT], F16, stack="p3C")
    acck = c.sb("acck", [64, TT], F16, stack="p3C")
    accv = c.sb("accv", [64, TT], F16, stack="p3C")
    accb = c.sb("accb", [64, TT], BF16, stack="p3C")
    tq0 = c.sb("tq0", [64, TT], F16, stack="p3C")
    tq1 = c.sb("tq1", [64, TT], F16, stack="p3C")

    _cast_weights(c, block, I, "qTw", qw, DC, D, stgC)
    _cast_weights(c, block, I, "kTw", kw, DC, D, stgC)
    _cast_weights(c, block, I, "vTw", vw, DC, D, stgC)

    @block.gpsimd
    def _(g):
        i = g.memset(onesw[:], 1.0)
        c.inc(i, "eO")

    @block.vector
    def _(ve):
        c.w(ve, "eO")
        for n in range(NST):
            i = ve.tensor_scalar(out=sigrep[:, n * 32:(n + 1) * 32], in0=onesw[:],
                                 scalar1=st["sig_bc"][:, n:n + 1], scalar2=None,
                                 op0=ALU.mult)
        c.inc(i, "eV")

    for j in range(NTILE):
        @block.sync
        def _(s, j=j):
            c.w(s, "eP"); c.w(s, "eA")
            ap = bass.AP(DD["d_fl"], j * D * TT, [[TT, 128], [128 * TT, DC], [1, TT]])
            s.dma_start(out=flc[:].rearrange("p (a t) -> p a t", t=TT), in_=ap
                        ).then_inc(c.sem["qS"], 16)
            c.cnt["qS"] += 16
            ap = bass.AP(DD["d_rn"], j * TT, [[1, 128], [128, 4]])
            i = s.dma_start(out=rn_pm[:, j * 4:(j + 1) * 4], in_=ap,
                            allow_slow_non_contiguous=True)
            c.inc(i, "qS", 16)

        for tg in range(4):
            gi = j * 4 + tg
            @block.tensor
            def _(te, j=j, tg=tg, gi=gi):
                c.w(te, "qS"); c.w(te, "eV"); c.w(te, "eA")
                for dc in range(DC):
                    i = te.matmul(st["ps0"][:, 0:NST],
                                  flc[:, dc * TT + tg * 128:dc * TT + (tg + 1) * 128],
                                  st["simn"][:, dc * NST:(dc + 1) * NST],
                                  start=(dc == 0), stop=(dc == DC - 1))
                c.inc(i, "eP")

            @block.vector
            def _(ve, gi=gi):
                c.w(ve, "eP")
                i = ve.tensor_scalar(
                    out=bass.AP(slog, gi, [[512, 128], [32, NST]]),
                    in0=st["ps0"][:, 0:NST],
                    scalar1=rn_pm[:, gi:gi + 1], scalar2=None, op0=ALU.mult)
                c.inc(i, "eV")

    def GV(eng):
        c.w(eng, "eV"); c.w(eng, "eO"); c.w(eng, "eA")

    nsl = lambda t, n: t[:, n * 32:(n + 1) * 32]
    gview = lambda t: t[:].rearrange("p (n g) -> p g n", g=32)
    negb = c.sb("negb", [128, 512], F32, stack="p3C")
    m1c = c.sb("m1c", [128, 32], F32, stack="p3C")
    m2c = c.sb("m2c", [128, 32], F32, stack="p3C")
    mxc = c.sb("mxc", [128, 32], F32, stack="p3C")
    rsc = c.sb("rsc", [128, 32], F32, stack="p3C")

    @block.gpsimd
    def _(g):
        GV(g)
        g.memset(negb[:], NEG)
        i = g.tensor_tensor(out=slog2[:], in0=slog[:], in1=sigrep[:], op=ALU.subtract)
        c.inc(i, "eO")

    @block.vector
    def _(ve):
        GV(ve)
        i = ve.tensor_reduce(out=m1[:], in_=gview(slog2), axis=mybir.AxisListType.X,
                             op=ALU.max)
        c.inc(i, "eV")

    @block.scalar
    def _(sc):
        GV(sc)
        i = sc.activation(out=m1c[:], in_=m1[:], func=AF.Copy)
        c.inc(i, "eA")

    @block.vector
    def _(ve):
        GV(ve)
        for n in range(NST):
            i = ve.tensor_tensor(out=nsl(eqt, n), in0=nsl(slog2, n), in1=m1c[:],
                                 op=ALU.is_equal)
        i2 = ve.tensor_scalar(out=activ[:], in0=m1c[:], scalar1=0.0, scalar2=None,
                              op0=ALU.is_gt)
        c.inc(i2, "eV")

    @block.gpsimd
    def _(g):
        GV(g)
        g.tensor_tensor(out=zb[:], in0=eqt[:], in1=negb[:], op=ALU.mult)
        i = g.tensor_tensor(out=zb[:], in0=zb[:], in1=slog2[:], op=ALU.add)
        c.inc(i, "eO")

    @block.vector
    def _(ve):
        GV(ve)
        ve.tensor_reduce(out=m2[:], in_=gview(zb), axis=mybir.AxisListType.X,
                         op=ALU.max)
        i = ve.tensor_scalar(out=kA[:], in0=slog2[:], scalar1=0.0, scalar2=None,
                             op0=ALU.is_gt)
        c.inc(i, "eV")

    @block.scalar
    def _(sc):
        GV(sc)
        i = sc.activation(out=m2c[:], in_=m2[:], func=AF.Copy)
        c.inc(i, "eA")

    @block.vector
    def _(ve):
        GV(ve)
        for n in range(NST):
            i = ve.tensor_tensor(out=nsl(kB, n), in0=nsl(slog2, n), in1=m2c[:],
                                 op=ALU.is_ge)
        c.inc(i, "eV")

    @block.gpsimd
    def _(g):
        GV(g)
        g.tensor_tensor(out=dif[:], in0=kA[:], in1=kB[:], op=ALU.subtract)
        for n in range(NST):
            g.tensor_tensor(out=nsl(keepm, n), in0=nsl(dif, n), in1=activ[:],
                            op=ALU.mult)
        i = g.tensor_tensor(out=keep[:], in0=keepm[:], in1=kB[:], op=ALU.add)
        c.inc(i, "eO")

    @block.vector
    def _(ve):
        GV(ve)
        ve.tensor_scalar(out=gated[:], in0=slog2[:], scalar1=0.0, scalar2=None,
                         op0=ALU.max)
        i = ve.tensor_scalar(out=t1g[:], in0=keep[:], scalar1=1.0e30, scalar2=NEG,
                             op0=ALU.mult, op1=ALU.add)
        c.inc(i, "eV")

    @block.gpsimd
    def _(g):
        GV(g)
        g.tensor_tensor(out=a_g[:], in0=gated[:], in1=keep[:], op=ALU.mult)
        i = g.tensor_tensor(out=maskd[:], in0=a_g[:], in1=t1g[:], op=ALU.add)
        c.inc(i, "eO")

    @block.vector
    def _(ve):
        GV(ve)
        i = ve.tensor_reduce(out=mx[:], in_=gview(maskd), axis=mybir.AxisListType.X,
                             op=ALU.max)
        c.inc(i, "eV")

    @block.scalar
    def _(sc):
        GV(sc)
        i = sc.activation(out=mxc[:], in_=mx[:], func=AF.Copy)
        c.inc(i, "eA")

    @block.vector
    def _(ve):
        GV(ve)
        for n in range(NST):
            i = ve.tensor_tensor(out=nsl(e_g, n), in0=nsl(maskd, n), in1=mxc[:],
                                 op=ALU.subtract)
        c.inc(i, "eV")

    @block.scalar
    def _(sc):
        GV(sc)
        i = sc.activation(out=rwf[:], in_=e_g[:], func=AF.Exp)
        c.inc(i, "eA")

    @block.vector
    def _(ve):
        GV(ve)
        i = ve.tensor_reduce(out=ssum[:], in_=gview(rwf), axis=mybir.AxisListType.X,
                             op=ALU.add)
        c.inc(i, "eV")

    @block.scalar
    def _(sc):
        GV(sc)
        i = sc.activation(out=ssum2[:], in_=ssum[:], func=AF.Copy)
        c.inc(i, "eA")

    @block.vector
    def _(ve):
        GV(ve)
        i = ve.reciprocal(out=rs_g[:], in_=ssum2[:])
        c.inc(i, "eV")

    @block.scalar
    def _(sc):
        GV(sc)
        i = sc.activation(out=rsc[:], in_=rs_g[:], func=AF.Copy)
        c.inc(i, "eA")

    @block.vector
    def _(ve):
        GV(ve)
        for n in range(NST):
            i = ve.tensor_tensor(out=nsl(e_g, n), in0=nsl(rwf, n), in1=rsc[:],
                                 op=ALU.mult)
        c.inc(i, "eV")

    @block.scalar
    def _(sc):
        GV(sc)
        i = sc.activation(out=rw_all[:], in_=e_g[:], func=AF.Copy)
        c.inc(i, "eA")

    for n in range(NST):
        @block.tensor
        def _(te, n=n):
            c.w(te, "eA")
            if n > 0:
                c.w(te, "qG")
            i = te.transpose(st["ps7"][0:32, 0:128], rw_all[:, n * 32:(n + 1) * 32],
                             st["ident"][:])
            c.inc(i, "eP")

        @block.scalar
        def _(sc, n=n):
            c.w(sc, "eP")
            i = sc.activation(out=rtr[:], in_=st["ps7"][0:32, 0:128], func=AF.Copy)
            c.inc(i, "eA")

        @block.gpsimd
        def _(g, n=n):
            c.w(g, "eA")
            ap = bass.AP(DD["d_rwT"], n * TH, [[128, 32], [1, 128]])
            i = g.dma_start(out=ap, in_=rtr[:])
            c.inc(i, "qG", 16)

    for j in range(NTILE):
        @block.sync
        def _(s, j=j):
            c.w(s, "eP"); c.w(s, "eA"); c.w(s, "eV"); c.w(s, "qG")
            ap = bass.AP(DD["d_fl"], j * D * TT, [[TT, 128], [128 * TT, DC], [1, TT]])
            i = s.dma_start(out=flc[:].rearrange("p (a t) -> p a t", t=TT), in_=ap)
            c.inc(i, "qS", 16)

        @block.gpsimd
        def _(g, j=j):
            c.w(g, "qG"); c.w(g, "eV")
            for e in range(NST):
                ap = bass.AP(DD["d_rwT"], e * TH + j * TT, [[0, 64], [1, TT]])
                i = g.dma_start(out=rwb[:, e * TT:(e + 1) * TT], in_=ap)
                c.inc(i, "qG", 16)

        for (p3, wmat, acc) in [(0, qw, accq), (1, kw, acck), (2, vw, accv)]:
            for ep in range(8):
                pb = ps2 if ep % 2 == 0 else ps3
                @block.tensor
                def _(te, p3=p3, wmat=wmat, ep=ep, pb=pb):
                    c.w(te, "qS"); c.w(te, "qG"); c.w(te, "eV")
                    for dc in range(DC):
                        i = te.matmul(pb[:], wmat[:, dc * D + ep * 128:dc * D + (ep + 1) * 128],
                                      flc[:, dc * TT:(dc + 1) * TT],
                                      start=(dc == 0), stop=(dc == DC - 1))
                    c.inc(i, "eP")

                @block.vector
                def _(ve, p3=p3, ep=ep, pb=pb, acc=acc):
                    c.w(ve, "eP")
                    ve.tensor_tensor(out=tq0[:], in0=pb[0:64, :],
                                     in1=rwb[:, (2 * ep) * TT:(2 * ep + 1) * TT], op=ALU.mult)
                    i = ve.tensor_tensor(out=tq1[:], in0=pb[64:128, :],
                                         in1=rwb[:, (2 * ep + 1) * TT:(2 * ep + 2) * TT],
                                         op=ALU.mult)
                    c.inc(i, "eV")

                @block.gpsimd
                def _(g, p3=p3, ep=ep, acc=acc):
                    c.w(g, "eV")
                    if ep == 0:
                        i = g.tensor_tensor(out=acc[:], in0=tq0[:], in1=tq1[:], op=ALU.add)
                    else:
                        g.tensor_tensor(out=acc[:], in0=acc[:], in1=tq0[:], op=ALU.add)
                        i = g.tensor_tensor(out=acc[:], in0=acc[:], in1=tq1[:], op=ALU.add)
                    c.inc(i, "eO")

            @block.scalar
            def _(sc, p3=p3, acc=acc):
                c.w(sc, "eO")
                i = sc.activation(out=accb[:], in_=acc[:], func=AF.Copy)
                c.inc(i, "eA")

            @block.gpsimd
            def _(g, p3=p3, j=j):
                c.w(g, "eA")
                if p3 == 0:
                    ap = bass.AP(DD["d_qLo"], j * TT, [[TH, 64], [1, TT]])
                else:
                    ap = bass.AP(DD["d_kvL"], (p3 - 1) * 64 * TH + j * TT,
                                 [[TH, 64], [1, TT]])
                i = g.dma_start(out=ap, in_=accb[:])
                c.inc(i, "qG", 16)

    @block.gpsimd
    def _(g):
        c.w(g, "qG")
        g.reg_alu(rS1, rH, TH, ALU.mult)
        g.dma_start(out=bass.AP(DD["s_kv"], rS1, [[T, 64], [1, TH]]),
                    in_=DD["d_kvL"].ap()[0]).then_inc(c.sem["qG"], 16)
        c.cnt["qG"] += 16
        g.reg_add(rS2, rS1, 64 * T)
        g.dma_start(out=bass.AP(DD["s_kv"], rS2, [[T, 64], [1, TH]]),
                    in_=DD["d_kvL"].ap()[1]).then_inc(c.sem["qG"], 16)
        c.cnt["qG"] += 16
        g.dma_start(out=bass.AP(DD["s_q"], rS1, [[T, 64], [1, TH]]),
                    in_=DD["d_qLo"].ap()).then_inc(c.sem["qG"], 16)
        c.cnt["qG"] += 16
        g.wait_ge(c.sem["qG"], c.cnt["qG"])
        g.reg_alu(rS2, rH, 8, ALU.mult)
        g.reg_add(rS2, rS2, FKV)
        i = g.dma_start(out=bass.AP(DD["s_fl"], rS2, [[1, 1], [1, 1]]),
                        in_=fvals[0:1, 1:2])
        c.inc(i, "qG", 16)
        g.wait_ge(c.sem["qG"], c.cnt["qG"])

    phase_wall(c, block)
    c.close_stack("p3C")


def emit_p4(c, block, I, O, DD, st):
    ident = st["ident"]
    fvals = st["fvals"]
    rH = st["gp_rH"]
    rS1, rS2 = st["gp_rS"]
    ps0, ps1, ps2, ps3, ps7 = st["ps0"], st["ps1"], st["ps2"], st["ps3"], st["ps7"]

    c.open_stack("p4")
    flp4 = c.sb("flp4", [1, 2], I32, stack="p4")
    k_all = c.sb("k_all", [64, T], BF16, stack="p4")
    v_all = c.sb("v_all", [64, T], BF16, stack="p4")
    vT = c.sb("vT", [128, 64 * 64], BF16, stack="p4")
    qv = c.sb("qv", [64, 32 * 128], BF16, stack="p4")
    mk = c.sb("mk", [128, 2 * TT], F32, stack="p4")
    pT = [c.sb(f"pT{i}", [128, TT], BF16, stack="p4") for i in range(2)]
    s_wk = c.sb("s_wk", [128, TT], F32, stack="p4")
    rowb = [c.sb(f"rowb{i}", [1, 128], F32, stack="p4") for i in range(2)]
    onesb = c.sb("onesb", [128, 1], BF16, stack="p4")
    rcp = [c.sb(f"rcp{i}", [128, 1], F32, stack="p4") for i in range(2)]
    o_sb = [c.sb(f"o_sb{i}", [128, 64], BF16, stack="p4") for i in range(2)]
    aof = c.sb("aof", [64, 32 * 128], BF16, stack="p4")

    emit_poll(c, block, DD, flp4, [(0, FKV, 1), (1, FKV, 1)], "p4kv")

    @block.sync
    def _(s):
        c.w(s, "eO")
        s.dma_start(out=k_all[:], in_=DD["s_kv"].ap()[0]).then_inc(c.sem["qS"], 16)
        s.dma_start(out=v_all[:], in_=DD["s_kv"].ap()[1]).then_inc(c.sem["qS"], 16)
        c.cnt["qS"] += 32
        rHs = st["sy_rH"]
        rs1, rs2 = st["sy_rS"]
        s.reg_alu(rs1, rHs, 128, ALU.mult)
        s.dma_start(out=qv[:].rearrange("p (i t) -> p i t", t=128),
                    in_=bass.AP(DD["s_q"], rs1, [[T, 64], [256, 32], [1, 128]])
                    ).then_inc(c.sem["qS"], 16)
        c.cnt["qS"] += 16
        s.reg_alu(rs2, rHs, 128 * TT, ALU.mult)
        s.dma_start(out=mk[:, 0:TT],
                    in_=bass.AP(I["masks"].tensor, rs2, [[TT, 128], [1, TT]])
                    ).then_inc(c.sem["qS"], 16)
        c.cnt["qS"] += 16
        s.reg_add(rs2, rs2, 2 * 128 * TT)
        i = s.dma_start(out=mk[:, TT:2 * TT],
                        in_=bass.AP(I["masks"].tensor, rs2, [[TT, 128], [1, TT]]))
        c.inc(i, "qS", 16)

    for grp in range(16):
        @block.tensor
        def _(te, grp=grp):
            c.w(te, "qS")
            if grp > 0:
                c.w(te, "eV")
            for u in range(4):
                n = grp * 4 + u
                i = te.transpose(ps7[0:128, u * 64:(u + 1) * 64],
                                 v_all[:, n * 128:(n + 1) * 128], ident[0:64, 0:64])
            c.inc(i, "eP")

        @block.vector
        def _(ve, grp=grp):
            c.w(ve, "eP")
            i = ve.tensor_copy(out=vT[:, grp * 256:(grp + 1) * 256],
                               in_=ps7[0:128, 0:256])
            c.inc(i, "eV")

    onesk, ones1 = st["onesk"], st["ones1"]

    @block.scalar
    def _(sc):
        i2 = sc.activation(out=onesb[:], in_=onesk[:], func=AF.Copy)
        c.inc(i2, "eA")
    ME_exp, ME_pv, ME_recip, ME_osb, ME_aof = {}, {}, {}, {}, {}
    lastexp = [0, 0]
    lastpv = [0, 0]
    lastdiag = [0]
    ME_aof[-1] = 0
    M_qload = c.cnt["qS"]

    def p4_score(i, kc, nch):
        bs = kc % 2
        psS = ps0 if bs == 0 else ps1
        diag = (kc == nch - 1)

        @block.tensor
        def _(te, i=i, kc=kc, psS=psS, bs=bs):
            te.wait_ge(c.sem["qS"], M_qload)
            c.w(te, "eA", lastexp[bs])
            for u in range(4):
                i2 = te.matmul(psS[:, u * 128:(u + 1) * 128],
                               k_all[:, kc * TT + u * 128:kc * TT + (u + 1) * 128],
                               qv[:, i * 128:(i + 1) * 128], start=True, stop=True)
            c.inc(i2, "eP")
        M_sc = c.cnt["eP"]

        M_msk = 0
        if diag:
            @block.vector
            def _(ve, i=i, psS=psS, M_sc=M_sc):
                mki = mk[:, (i % 2) * TT:(i % 2 + 1) * TT]
                c.w(ve, "eP", M_sc)
                c.w(ve, "eA", lastdiag[0])
                i2 = ve.tensor_tensor(out=s_wk[:], in0=psS[:], in1=mki, op=ALU.add)
                c.inc(i2, "eV")
            M_msk = c.cnt["eV"]

        @block.scalar
        def _(sc, bs=bs, diag=diag, psS=psS, M_sc=M_sc, M_msk=M_msk):
            c.w(sc, "eP", max(M_sc, lastpv[bs]))
            if diag:
                c.w(sc, "eV", M_msk)
            src = s_wk[:] if diag else psS[:]
            i2 = sc.activation(out=pT[bs][:], in_=src, func=AF.Exp, scale=0.125)
            c.inc(i2, "eA")
        ME_exp[(i, kc)] = c.cnt["eA"]
        lastexp[bs] = ME_exp[(i, kc)]
        if diag:
            lastdiag[0] = ME_exp[(i, kc)]

    def p4_pv(i, kc, nch):
        bs = kc % 2
        r = i % 2

        @block.tensor
        def _(te, i=i, kc=kc, bs=bs, nch=nch, r=r):
            c.w(te, "eA", ME_exp[(i, kc)])
            if kc == 0 and i >= 2:
                c.w(te, "eV", ME_recip[i - 2])
                c.w(te, "eA", max(ME_exp[(i, kc)], ME_osb[i - 2]))
            for u in range(4):
                te.matmul(ps2[0:128, r * 64:r * 64 + 64],
                          pT[bs][:, u * 128:(u + 1) * 128],
                          vT[:, (kc * 4 + u) * 64:(kc * 4 + u + 1) * 64],
                          start=(kc == 0 and u == 0),
                          stop=(kc == nch - 1 and u == 3))
                i2 = te.matmul(ps3[0:1, r * 128:r * 128 + 128], onesb[:],
                               pT[bs][:, u * 128:(u + 1) * 128],
                               start=(kc == 0 and u == 0),
                               stop=(kc == nch - 1 and u == 3))
            c.inc(i2, "eP")
        ME_pv[(i, kc)] = c.cnt["eP"]
        lastpv[bs] = ME_pv[(i, kc)]

    def p4_tail(i, nch):
        r = i % 2

        @block.vector
        def _(ve, i=i, r=r):
            c.w(ve, "eP", ME_pv[(i, nch - 1)])
            i2 = ve.reciprocal(out=rowb[r][:], in_=ps3[0:1, r * 128:r * 128 + 128])
            c.inc(i2, "eV")
        ME_recip[i] = c.cnt["eV"]

        @block.tensor
        def _(te, i=i, r=r):
            c.w(te, "eV", ME_recip[i])
            i2 = te.matmul(ps3[0:128, 256 + r:257 + r], rowb[r][:], ones1[0:1, 0:1],
                           start=True, stop=True)
            c.inc(i2, "eP")
        M_k1 = c.cnt["eP"]

        @block.scalar
        def _(sc, i=i, r=r, M_k1=M_k1):
            c.w(sc, "eP", M_k1)
            sc.activation(out=rcp[r][:], in_=ps3[0:128, 256 + r:257 + r], func=AF.Copy)
            i2 = sc.activation(out=o_sb[r][:], in_=ps2[0:128, r * 64:r * 64 + 64],
                               func=AF.Identity, scale=rcp[r][:])
            c.inc(i2, "eA")
        ME_osb[i] = c.cnt["eA"]

        @block.tensor
        def _(te, i=i, r=r):
            c.w(te, "eA", ME_osb[i])
            c.w(te, "eV", ME_aof[i - 1])
            i2 = te.transpose(ps7[0:64, r * 128:(r + 1) * 128], o_sb[r][:, 0:64],
                              ident[:])
            c.inc(i2, "eP")
        M_tr2 = c.cnt["eP"]

        @block.vector
        def _(ve, i=i, r=r, M_tr2=M_tr2):
            c.w(ve, "eP", M_tr2)
            i2 = ve.tensor_copy(out=aof[:, i * 128:(i + 1) * 128],
                                in_=ps7[0:64, r * 128:(r + 1) * 128])
            c.inc(i2, "eV")
        ME_aof[i] = c.cnt["eV"]

    ME_osb[-1] = ME_osb[0 - 2] = 0
    ME_recip[-1] = ME_recip[-2] = 0
    pend = None
    for i in range(32):
        nch = i // 2 + 1
        for kc in range(nch):
            p4_score(i, kc, nch)
            if kc == 0 and pend is not None:
                p4_tail(*pend)
                pend = None
            if kc >= 1:
                p4_pv(i, kc - 1, nch)
        p4_pv(i, nch - 1, nch)
        pend = (i, nch)
    p4_tail(*pend)

    @block.gpsimd
    def _(g):
        c.w(g, "eV"); c.w(g, "qS")
        g.reg_alu(rS1, rH, 128, ALU.mult)
        g.dma_start(out=bass.AP(DD["s_ao"], rS1, [[T, 64], [256, 32], [1, 128]]),
                    in_=aof[:].rearrange("p (i t) -> p i t", t=128)
                    ).then_inc(c.sem["qG"], 16)
        c.cnt["qG"] += 16
        g.wait_ge(c.sem["qG"], c.cnt["qG"])
        g.reg_alu(rS2, rH, 8, ALU.mult)
        g.reg_add(rS2, rS2, FAO)
        i = g.dma_start(out=bass.AP(DD["s_fl"], rS2, [[1, 1], [1, 1]]),
                        in_=fvals[0:1, 1:2])
        c.inc(i, "qG", 16)
        g.wait_ge(c.sem["qG"], c.cnt["qG"])

    phase_wall(c, block)
    c.close_stack("p4")


def emit_p5(c, block, I, O, DD, st):
    fvals = st["fvals"]
    ps2, ps3 = st["ps2"], st["ps3"]

    c.open_stack("p5")
    flp5 = c.sb("flp5", [1, 2], I32, stack="p5")
    ow2 = c.sb("ow2", [128, 8 * D], BF16, stack="p5")
    stg5 = c.sb("stg5", [128, D], F32, stack="p5")
    aot = [c.sb(f"aot{i}", [64, TT], BF16, stack="p5") for i in range(2)]
    rwb5 = [c.sb(f"rwb5{i}", [64, NST * TT], BF16, stack="p5") for i in range(2)]
    m2 = [c.sb(f"m2p5{i}", [128, 8 * TT], BF16, stack="p5") for i in range(2)]
    x1l = [c.sb(f"x1l{i}", [128, DC * TT], F32, stack="p5") for i in range(2)]
    yo = [c.sb(f"yo{i}", [128, DC * TT], F32, stack="p5") for i in range(2)]

    emit_poll(c, block, DD, flp5, [(0, FAO, 1), (1, FAO, 1)], "p5ao")

    @block.sync
    def _(s):
        c.w(s, "eO")
        rHs = st["sy_rH"]
        rs1, rs2 = st["sy_rS"]
        s.reg_alu(rs1, rHs, TH, ALU.mult)
        i = s.dma_start(out=DD["d_aoR"].ap(),
                        in_=bass.AP(DD["s_ao"], rs1, [[T, 64], [1, TH]]))
        c.inc(i, "qS", 16)

    _cast_weights(c, block, I, "oTw", ow2, 8, D, stg5)

    P5_ld, P5_rwb, P5_m2, P5_mm, P5_ev, P5_out = {}, {}, {}, {}, {}, {}
    M_owrdy = c.cnt["eA"]
    for j in range(NTILE):
        b = j % 2

        @block.sync
        def _(s, j=j, b=b):
            if j >= 2:
                c.w(s, "eV", P5_ev[(j - 2, 7)])
            s.dma_start(out=aot[b][:], in_=DD["d_aoR"].ap()[:, j * TT:(j + 1) * TT]
                        ).then_inc(c.sem["qS"], 16)
            ap = bass.AP(DD["d_x1"], j * D * TT, [[TT, 128], [128 * TT, DC], [1, TT]])
            i = s.dma_start(out=x1l[b][:].rearrange("p (a t) -> p a t", t=TT), in_=ap)
            c.cnt["qS"] += 16
            c.inc(i, "qS", 16)
        P5_ld[j] = c.cnt["qS"]

        @block.gpsimd
        def _(g, j=j, b=b):
            if j >= 2:
                c.w(g, "eV", P5_m2[j - 2])
            for e in range(NST):
                ap = bass.AP(DD["d_rwT"], e * TH + j * TT, [[0, 64], [1, TT]])
                i = g.dma_start(out=rwb5[b][:, e * TT:(e + 1) * TT], in_=ap)
                c.inc(i, "qG", 16)
        P5_rwb[j] = c.cnt["qG"]

        @block.vector
        def _(ve, j=j, b=b):
            ve.wait_ge(c.sem["qS"], P5_ld[j])
            ve.wait_ge(c.sem["qG"], P5_rwb[j])
            if j >= 2:
                c.w(ve, "eP", P5_mm[(j - 2, 7)])
            for pr in range(8):
                ve.tensor_tensor(out=m2[b][0:64, pr * TT:(pr + 1) * TT], in0=aot[b][:],
                                 in1=rwb5[b][:, (2 * pr) * TT:(2 * pr + 1) * TT],
                                 op=ALU.mult)
                i = ve.tensor_tensor(out=m2[b][64:128, pr * TT:(pr + 1) * TT],
                                     in0=aot[b][:],
                                     in1=rwb5[b][:, (2 * pr + 1) * TT:(2 * pr + 2) * TT],
                                     op=ALU.mult)
            c.inc(i, "eV")
        P5_m2[j] = c.cnt["eV"]

        for oc in range(DC):
            pb = ps2 if oc % 2 == 0 else ps3
            @block.tensor
            def _(te, j=j, b=b, oc=oc, pb=pb):
                c.w(te, "eV", P5_m2[j])
                c.w(te, "eA", M_owrdy)
                if oc >= 2:
                    c.w(te, "eV", P5_ev[(j, oc - 2)])
                elif j >= 1:
                    c.w(te, "eV", P5_ev[(j - 1, 6 + oc)])
                for pr in range(8):
                    i = te.matmul(pb[:], ow2[:, pr * D + oc * 128:pr * D + (oc + 1) * 128],
                                  m2[b][:, pr * TT:(pr + 1) * TT],
                                  start=(pr == 0), stop=(pr == 7))
                c.inc(i, "eP")
            P5_mm[(j, oc)] = c.cnt["eP"]

            @block.vector
            def _(ve, j=j, b=b, oc=oc, pb=pb):
                c.w(ve, "eP", P5_mm[(j, oc)])
                i = ve.tensor_tensor(out=yo[b][:, oc * TT:(oc + 1) * TT], in0=pb[:],
                                     in1=x1l[b][:, oc * TT:(oc + 1) * TT], op=ALU.add)
                c.inc(i, "eV")
            P5_ev[(j, oc)] = c.cnt["eV"]

        @block.gpsimd
        def _(g, j=j, b=b):
            c.w(g, "eV", P5_ev[(j, 7)])
            ap = bass.AP(O["yout"].tensor, j * TT, [[TH, 128], [128 * TH, DC], [1, TT]])
            i = g.dma_start(out=ap, in_=yo[b][:].rearrange("p (a t) -> p a t", t=TT))
            c.inc(i, "qG", 16)
        P5_out[j] = c.cnt["qG"]

    phase_wall(c, block)
    c.close_stack("p5")


def emit_tail(c, block, I, O, DD, st):
    @block.gpsimd
    def _(g):
        c.wall(g)
        for name in _DEBUG_TAPS:
            t = DD[name]
            i = g.dma_start(out=O["dbg_" + name], in_=t.ap())
            c.inc(i, "qG", 16)
        i = g.dma_start(out=O["meta"], in_=st["meta_sb"][:])
        c.inc(i, "qG", 16)
        g.wait_ge(c.sem["qG"], c.cnt["qG"])



def _prep_inputs(inp):
    import ml_dtypes
    f = np.float32
    x = np.asarray(inp["L0_attn_x"], f)
    xT = np.ascontiguousarray(x.transpose(0, 2, 1))
    xhalo = np.ascontiguousarray(x[:, TH - 1:TH + 2, :])

    sm = np.zeros((128, 256), f)

    def put(col, vec1024):
        v = np.asarray(vec1024, f).reshape(DC, 128)
        for dc in range(DC):
            sm[:, col + dc] = v[dc]

    put(SM["ln1g"], inp["L0_attn_ln1_g"])
    put(SM["ln1b"], inp["L0_attn_ln1_b"])
    put(SM["ln2g"], inp["L0_attn_ln2_g"])
    put(SM["ln2b"], inp["L0_attn_ln2_b"])
    put(SM["dtb"], inp["L0_attn_dt_proj_b"])
    put(SM["Dp"], inp["L0_attn_Dp"])
    put(SM["cxb"], inp["L0_attn_conv_x_b"])
    put(SM["czb"], inp["L0_attn_conv_z_b"])
    put(SM["b2"], inp["L0_attn_mlp_b2"])
    b1 = np.asarray(inp["L0_attn_mlp_b1"], f).reshape(32, 128)
    for cb in range(32):
        sm[:, SM["b1"] + cb] = b1[cb]
    cwx = np.asarray(inp["L0_attn_conv_x_w"], f)[:, 0, :]
    cwz = np.asarray(inp["L0_attn_conv_z_w"], f)[:, 0, :]
    for k in range(4):
        put(SM["cwx"] + 8 * k, cwx[:, k])
        put(SM["cwz"] + 8 * k, cwz[:, k])

    masks = np.zeros((4, 128, TT), f)
    qidx = np.arange(128)
    for m in range(4):
        for sub in range(4):
            for p in range(128):
                bad = (sub * 128 + p) > (m * 128 + qidx)
                masks[m, p, sub * 128:(sub + 1) * 128][bad] = NEG
    hvalid = np.array([[0.0, 1.0, 1.0, 0.0], [1.0, 0.0, 0.0, 0.0]], f)

    return dict(
        xT=xT,
        xhalo=xhalo,
        w_in_T=np.ascontiguousarray(np.asarray(inp["L0_attn_in_proj_w"], f).T),
        smallpm=sm,
        xproj_T=np.ascontiguousarray(np.asarray(inp["L0_attn_x_proj_w"], f).T),
        dtproj_T=np.ascontiguousarray(np.asarray(inp["L0_attn_dt_proj_w"], f).T),
        A_log=np.ascontiguousarray(np.asarray(inp["L0_attn_A_log"], f)),
        outT=np.ascontiguousarray(np.asarray(inp["L0_attn_out_proj_w"], f).T),
        w1T=np.ascontiguousarray(np.asarray(inp["L0_attn_mlp_w1"], f).T),
        w2T=np.ascontiguousarray(np.asarray(inp["L0_attn_mlp_w2"], f).T),
        sim=np.ascontiguousarray(np.asarray(inp["L0_ffn_sim_matrix"], f)),
        gates=np.ascontiguousarray(np.asarray(inp["L0_ffn_gates"], f)[None, :]),
        qTw=np.ascontiguousarray(np.asarray(inp["L0_ffn_q_proj"], f).transpose(1, 0, 2).reshape(D, D)),
        kTw=np.ascontiguousarray(np.asarray(inp["L0_ffn_k_proj"], f).transpose(1, 0, 2).reshape(D, D)),
        vTw=np.ascontiguousarray(np.asarray(inp["L0_ffn_v_proj"], f).transpose(1, 0, 2).reshape(D, D)),
        oTw=np.ascontiguousarray(np.asarray(inp["L0_ffn_o_proj"], f).reshape(D, D)),
        masks=masks,
        ident=np.eye(128, dtype=f).astype(ml_dtypes.bfloat16),
        hvalid=hvalid,
        eye4=np.eye(4, dtype=f),
    )


def kernel(**inputs):
    ins = _prep_inputs(inputs)
    nc = build_nc()
    res = run_bass_kernel_spmd(nc, [ins] * 8, core_ids=list(range(8)))
    out = np.zeros((NB, D, T), np.float32)
    seen = set()
    for r in res.results:
        b = int(round(float(r["meta"][0, 0])))
        h = int(round(float(r["meta"][0, 1])))
        out[b, :, h * TH:(h + 1) * TH] = r["yout"]
        seen.add((b, h))
    assert len(seen) == 8, f"missing shards: {sorted(seen)}"
    return np.ascontiguousarray(out.transpose(0, 2, 1))
